# version 06
# baseline (speedup 1.0000x reference)
import contextlib
import numpy as np
import concourse.bass as bass
import concourse.mybir as mybir
from concourse.bass_utils import run_bass_kernel_spmd

F32 = mybir.dt.float32
BF16 = mybir.dt.bfloat16
AF = mybir.ActivationFunctionType
ALU = mybir.AluOpType
AXX = mybir.AxisListType.X

D, DC, L, T, NT, E, HD = 2048, 16, 4096, 512, 8, 8, 128
NBUF, DEPTH = 6, 4
GPT = E * 48
NEG = -1.0e30
ISQ = float(1.0 / np.sqrt(128.0))


class Ctr:
    def __init__(self):
        self.v = 0

    def inc(self, n=1):
        self.v += n
        return self.v


class Prog:
    def __init__(self):
        self.q = {k: [] for k in ("t", "v", "s", "y", "g")}
        self.vsem = None
        self.vctr = None
        self.marks = []
        self.cur = "init"
        self.tags = {k: [] for k in ("t", "v", "s", "y", "g")}

    def emit(self, eng, f):
        self.q[eng].append(f)
        self.tags[eng].append(self.cur)

    def wait(self, eng, sem, val):
        if val > 0:
            self.q[eng].append(lambda e, sem=sem, val=val: e.wait_ge(sem, val))
            self.tags[eng].append(self.cur + "|W")

    def mark(self, label):
        self.cur = label
        self.marks.append((label, {k: len(v) for k, v in self.q.items()}))

    def vstep(self, f):
        sem = self.vsem
        self.q["v"].append(lambda e, f=f, sem=sem: f(e).then_inc(sem, 1))
        self.tags["v"].append(self.cur + "|VS")
        tgt = self.vctr.inc()
        self.q["v"].append(lambda e, sem=sem, tgt=tgt: e.wait_ge(sem, tgt))
        self.tags["v"].append(self.cur + "|VSW")
        return tgt


class _MockIns:
    def __init__(self, sim):
        self.sim = sim

    def then_inc(self, sem, n=1):
        self.sim.pending.append((id(sem), n))
        return self


class _MockEng:
    def __init__(self, sim):
        self.sim = sim

    def wait_ge(self, sem, val):
        self.sim.waits.append((id(sem), val))
        return _MockIns(self.sim)

    def __getattr__(self, name):
        return lambda *a, **k: _MockIns(self.sim)


def simulate(p, semnames):
    class S:
        pending, waits = [], []
    sim = S()
    sems, ptrs = {}, {k: 0 for k in p.q}
    engs = {k: _MockEng(sim) for k in p.q}
    progress = True
    while progress:
        progress = False
        for k in p.q:
            while ptrs[k] < len(p.q[k]):
                sim.waits, sim.pending = [], []
                p.q[k][ptrs[k]](engs[k])
                if not all(sems.get(s_, 0) >= v for s_, v in sim.waits):
                    break
                for s_, n in sim.pending:
                    sems[s_] = sems.get(s_, 0) + n
                ptrs[k] += 1
                progress = True
    stuck = {k: (ptrs[k], len(p.q[k])) for k in p.q if ptrs[k] < len(p.q[k])}
    if stuck:
        for k in stuck:
            around = [p.tags[k][i] if 0 <= i < len(p.tags[k]) else "?"
                      for i in range(ptrs[k] - 2, ptrs[k] + 3)]
            print(f"stuck {k}@{ptrs[k]}: tags around {around}")
        det = {}
        for k in stuck:
            sim.waits, sim.pending = [], []
            p.q[k][ptrs[k]](engs[k])
            det[k] = [(semnames.get(s_, "?"), v, sems.get(s_, 0)) for s_, v in sim.waits]
        raise RuntimeError(f"program deadlock: {stuck} {det}")


def build_nc(debug=False):
    nc = bass.Bass("TRN2", target_bir_lowering=False)
    dr = lambda n, shp, dt=F32, kind="ExternalInput": nc.dram_tensor(n, shp, dt, kind=kind).ap()
    x_in = dr("x_in", [L, D])
    wsw = dr("wsw", [GPT, 128, 2048])
    gw_in = dr("gw_in", [128, DC * 7])
    gbb_in = dr("gbb_in", [128, 8])
    bbb_in = dr("bbb_in", [128, 8])
    wq_in = dr("wq_in", [128, DC * HD])
    wk_in = dr("wk_in", [128, DC * HD])
    wv_in = dr("wv_in", [128, DC * HD])
    wo_in = dr("wo_in", [128, D])
    bqkv_in = dr("bqkv_in", [128, 4])
    bo_in = dr("bo_in", [128, DC])
    ones_in = dr("ones_in", [128, 128])
    onescol_in = dr("onescol_in", [128, 2])
    ident_in = dr("ident_in", [128, 128])
    masks_in = dr("masks_in", [128, 4 * T])
    y_out = dr("y_out", [L, D], kind="ExternalOutput")
    xf = dr("xf", [DC, 128, L], kind="Internal")
    h1 = dr("h1", [DC, 128, L], kind="Internal")
    h2 = dr("h2", [DC, 128, L], kind="Internal")
    wbf = dr("wbf", [GPT, 128, 2048], BF16, kind="Internal")
    if debug:
        dbg_h1 = dr("dbg_h1", [DC, 128, L], kind="ExternalOutput")
        dbg_h2 = dr("dbg_h2", [DC, 128, L], kind="ExternalOutput")
        dbg_xf = dr("dbg_xf", [DC, 128, L], kind="ExternalOutput")
        dbg_acc = dr("dbg_acc", [DC, 128, L], kind="ExternalOutput")
        dbg_e = dr("dbg_e", [E, DC, 128, T], kind="ExternalOutput")

    with contextlib.ExitStack() as ctx:
        sb = lambda n, shp, dt: ctx.enter_context(nc.sbuf_tensor(n, shp, dt))
        psa = lambda n, shp, dt=F32: ctx.enter_context(nc.psum_tensor(n, shp, dt))
        sem = lambda n: ctx.enter_context(nc.semaphore(n))

        wstream = sb("wstream", [128, NBUF * 2048], BF16)
        xn = sb("xn", [128, 2 * DC * T], BF16)
        hp = sb("hp", [128, DC * T], BF16)
        acc = sb("acc", [128, DC * T], F32)
        sch = sb("sch", [128, 2 * T], BF16)
        t1ch = sb("t1ch", [128, 2 * T], BF16)
        xch = sb("xch", [128, 4 * T], F32)
        x2ch = sb("x2ch", [128, 2 * T], F32)
        otb = sb("otb", [128, 2 * 2048], F32)
        rows = sb("rows", [1, 6 * T], F32)
        mub = sb("mub", [128, T], F32)
        rb = sb("rb", [128, T], F32)
        gwsb = sb("gwsb", [128, DC * 7], BF16)
        gbc = sb("gbc", [128, 8], F32)
        bbc = sb("bbc", [128, 8], F32)
        gsc = sb("gsc", [128, 16], F32)
        gt = sb("gt", [128, 8 * 7], F32)
        wTsb = sb("wTsb", [7, 4 * 128], F32)
        wbt = sb("wbt", [128, 2 * 8 * T], BF16)
        onescol = sb("onescol", [128, 2], F32)
        onesrow_b = sb("onesrow_b", [1, 128], BF16)
        onesrow_f = sb("onesrow_f", [1, 128], F32)
        ident = sb("ident", [128, 128], F32)
        identb = sb("identb", [128, 128], BF16)
        masks = sb("masks", [128, 4 * T], BF16)
        wqkv = sb("wqkv", [128, 3 * D], BF16)
        wo = sb("wo", [128, D], BF16)
        bqkv = sb("bqkv", [128, 4], F32)
        bo = sb("bo", [128, DC], F32)
        arow = sb("arow", [128, 8], F32)

        qsb = wstream[:, 0:L]
        ksb = wstream[:, L:2 * L]
        vT = wstream[:, 2 * L:3 * L]
        wrow = sb("wrow", [1, 7 * T], BF16)

        ps_rot = psa("ps_rot", [128, 4 * 512])
        ps_tr = psa("ps_tr", [128, 512])
        ps_trb = psa("ps_trb", [128, 1024], BF16)
        ps_wa = psa("ps_wa", [128, 512])
        ps_wb = psa("ps_wb", [128, 512])

        s_row = sem("s_row")
        s_mm, s_sc, s_ve = sem("s_mm"), sem("s_sc"), sem("s_ve")
        s_cst = sem("s_cst")
        s_acst = sem("s_acst")
        s_dbg = sem("s_dbg")
        s_racc = sem("s_racc")
        SC = [sem(f"s_xc{i}") for i in range(4)]
        SW = [sem(f"s_wl{i}") for i in range(NBUF)]
        STE = [sem(f"s_te{i}") for i in range(NBUF)]
        SO = [sem(f"s_ot{i}") for i in range(2)]
        SY = [sem(f"s_yo{i}") for i in range(2)]
        SX2 = [sem(f"s_x2{i}") for i in range(2)]

        p = Prog()
        c_row, c_mm, c_sc, c_ve = (Ctr() for _ in range(4))
        c_cst, c_acst, c_dbg, c_racc = Ctr(), Ctr(), Ctr(), Ctr()
        cSC = [Ctr() for _ in range(4)]
        cSW = [Ctr() for _ in range(NBUF)]
        cSTE = [Ctr() for _ in range(NBUF)]
        cSO = [Ctr() for _ in range(2)]
        cSY = [Ctr() for _ in range(2)]
        cSX2 = [Ctr() for _ in range(2)]
        p.vsem, p.vctr = s_ve, c_ve

        slot_free = [None] * 4
        slot_ctr = [0]

        def take_slot():
            sl = slot_ctr[0] % 4
            slot_ctr[0] += 1
            if slot_free[sl] is not None:
                p.wait("t", *slot_free[sl])
            return sl

        rot = lambda sl: ps_rot[:, sl * 512:(sl + 1) * 512]

        xnv = lambda b: xn[:, b * DC * T:(b + 1) * DC * T]
        wbtv = lambda b: wbt[:, b * 8 * T:(b + 1) * 8 * T]
        GT = {n: gt[:, i * 7:(i + 1) * 7] for i, n in
              enumerate(["log", "pp", "b", "eqm", "b2", "mask", "pg", "w05"])}
        COL = {n: gsc[:, i:i + 1] for i, n in enumerate(["m1", "m2", "m3", "sum", "r"])}
        ROWS = {n: rows[0:1, i * T:(i + 1) * T] for i, n in
                enumerate(["mu", "var", "sig", "rstd", "tmp", "tmp2"])}

        mm_of_group = {}
        wld_tgt, tee_tgt = {}, {}
        t1_of_par = {}

        for f in [
            lambda e: e.dma_start(out=onescol[:], in_=onescol_in),
            lambda e: e.dma_start(out=ident[:], in_=ident_in),
            lambda e: e.dma_start(out=gbc[:], in_=gbb_in),
            lambda e: e.dma_start(out=bbc[:], in_=bbb_in),
            lambda e: e.dma_start(out=bqkv[:], in_=bqkv_in),
            lambda e: e.dma_start(out=bo[:], in_=bo_in),
            lambda e: e.dma_start(out=onesrow_f[:], in_=ones_in[0:1, :]),
        ]:
            p.emit("y", lambda e, f=f: f(e).then_inc(s_cst, 16))
            c_cst.inc(16)
        for f in [
            lambda e: e.dma_start(out=masks[:], in_=masks_in),
            lambda e: e.dma_start(out=gwsb[:], in_=gw_in),
            lambda e: e.dma_start(out=onesrow_b[:], in_=ones_in[0:1, :]),
            lambda e: e.dma_start(out=identb[:], in_=ident_in),
            lambda e: e.dma_start(out=wqkv[:, 0:D], in_=wq_in),
            lambda e: e.dma_start(out=wqkv[:, D:2 * D], in_=wk_in),
            lambda e: e.dma_start(out=wqkv[:, 2 * D:3 * D], in_=wv_in),
            lambda e: e.dma_start(out=wo[:], in_=wo_in),
        ]:
            p.emit("g", lambda e, f=f: f(e).then_inc(s_cst, 16))
            c_cst.inc(16)
        consts_ld = c_cst.v
        p.vstep(lambda e: e.memset(wbt[:, 0:T], 0.5))
        ve_wbt0 = p.vstep(lambda e: e.memset(wbt[:, 8 * T:9 * T], 0.5))

        p.wait("t", s_cst, consts_ld)
        ent_mm, ent_ev, ent_st = {}, {}, {}
        for tc in range(32):
            ob = tc % 2

            def lde(e, tc=tc, ob=ob):
                if tc >= 2:
                    e.wait_ge(s_mm, ent_mm[tc - 2])
                e.dma_start(out=otb[:, ob * 2048:(ob + 1) * 2048],
                            in_=x_in[tc * 128:(tc + 1) * 128, :]).then_inc(SO[ob], 16)
            p.emit("y", lde)
            cSO[ob].inc(16)
            p.wait("t", SO[ob], cSO[ob].v)
            for dcg in range(4):
                if (tc, dcg) != (0, 0):
                    prev = ent_ev.get((tc, dcg - 1), ent_ev.get((tc - 1, 3)))
                    p.wait("t", s_ve, prev)
                for j in range(4):
                    dc = dcg * 4 + j

                    def trf(e, ob=ob, dc=dc, j=j, fin=(j == 3)):
                        ins = e.transpose(ps_tr[:, j * 128:(j + 1) * 128],
                                          otb[:, ob * 2048 + dc * 128: ob * 2048 + (dc + 1) * 128],
                                          ident[:])
                        if fin:
                            ins.then_inc(s_mm, 1)
                    p.emit("t", trf)
                c_mm.inc()
                ent_mm[tc] = c_mm.v
                p.wait("v", s_mm, c_mm.v)
                ob2 = dcg % 2
                if ent_st.get((tc, dcg)) is None:
                    pass
                prev_key = (tc, dcg - 2) if dcg >= 2 else (tc - 1, dcg + 2)
                if prev_key in ent_st:
                    p.wait("v", SX2[ob2], ent_st[prev_key])
                ent_ev[(tc, dcg)] = p.vstep(lambda e, ob2=ob2: e.tensor_copy(
                    x2ch[:, ob2 * T:(ob2 + 1) * T], ps_tr[:]))
                for j in range(4):
                    def ste(e, tc=tc, dcg=dcg, j=j, ob2=ob2, tgt=ent_ev[(tc, dcg)]):
                        if j == 0:
                            e.wait_ge(s_ve, tgt)
                        e.dma_start(out=xf[dcg * 4 + j][:, tc * 128:(tc + 1) * 128],
                                    in_=x2ch[:, ob2 * T + j * 128:
                                             ob2 * T + (j + 1) * 128]).then_inc(SX2[ob2], 16)
                    p.emit("y", ste)
                    cSX2[ob2].inc(16)
                ent_st[(tc, dcg)] = cSX2[ob2].v

        def ln_tile(src, tile, xb):
            mm_now, ve_now = c_mm.v, c_ve.v
            x2w = [cSX2[0].v, cSX2[1].v]
            mm_stats, sq_at = {}, {}
            sl_a, sl_b = take_slot(), take_slot()
            for dc in range(DC):
                b4, b2 = dc % 4, dc % 2

                def ldx(e, dc=dc, b4=b4):
                    if dc == 0:
                        e.wait_ge(s_mm, mm_now)
                        e.wait_ge(s_ve, ve_now)
                    if dc >= 4:
                        e.wait_ge(s_mm, mm_stats[dc - 4])
                        e.wait_ge(s_sc, sq_at[dc - 4])
                    e.dma_start(out=xch[:, b4 * T:(b4 + 1) * T],
                                in_=src[dc][:, tile * T:(tile + 1) * T]).then_inc(SC[b4], 16)
                p.emit("y", ldx)
                cSC[b4].inc(16)
                ld_tgt = cSC[b4].v
                if dc == 0:
                    p.wait("s", s_ve, ve_now)
                    p.wait("s", SX2[0], x2w[0])
                    p.wait("s", SX2[1], x2w[1])
                p.wait("s", SC[b4], ld_tgt)
                if dc >= 2:
                    p.wait("s", s_mm, mm_stats[dc - 2])
                p.emit("s", lambda e, b4=b4, b2=b2: e.activation(
                    x2ch[:, b2 * T:(b2 + 1) * T], xch[:, b4 * T:(b4 + 1) * T],
                    AF.Square).then_inc(s_sc, 1))
                c_sc.inc()
                sq_at[dc] = c_sc.v
                p.wait("t", SC[b4], ld_tgt)
                p.emit("t", lambda e, b4=b4, dc=dc, sl_a=sl_a: e.matmul(
                    rot(sl_a)[0:1, :], onescol[:, 0:1], xch[:, b4 * T:(b4 + 1) * T],
                    start=(dc == 0), stop=(dc == DC - 1)))
                p.wait("t", s_sc, c_sc.v)
                p.emit("t", lambda e, b2=b2, dc=dc, sl_b=sl_b: e.matmul(
                    rot(sl_b)[0:1, :], onescol[:, 0:1], x2ch[:, b2 * T:(b2 + 1) * T],
                    start=(dc == 0), stop=(dc == DC - 1)).then_inc(s_mm, 1))
                c_mm.inc()
                mm_stats[dc] = c_mm.v
            p.wait("v", s_mm, c_mm.v)
            p.vstep(lambda e: e.tensor_scalar(ROWS["mu"], rot(sl_a)[0:1, :],
                                              1.0 / D, None, op0=ALU.mult))
            p.vstep(lambda e: e.tensor_scalar(ROWS["var"], rot(sl_b)[0:1, :],
                                              1.0 / D, 1e-5, op0=ALU.mult, op1=ALU.add))
            p.vstep(lambda e: e.tensor_tensor(ROWS["tmp"], ROWS["mu"], ROWS["mu"], op=ALU.mult))
            tgt = p.vstep(lambda e: e.tensor_tensor(ROWS["var"], ROWS["var"], ROWS["tmp"],
                                                    op=ALU.subtract))
            slot_free[sl_a] = slot_free[sl_b] = (s_ve, tgt)
            p.wait("s", s_ve, tgt)
            p.emit("s", lambda e: e.activation(ROWS["sig"], ROWS["var"], AF.Sqrt).then_inc(s_sc, 1))
            c_sc.inc()
            p.wait("v", s_sc, c_sc.v)
            ve_r = p.vstep(lambda e: e.reciprocal(ROWS["rstd"], ROWS["sig"]))
            sl_c, sl_d = take_slot(), take_slot()
            p.wait("t", s_ve, ve_r)
            p.emit("t", lambda e: e.matmul(rot(sl_c), onesrow_f[:], ROWS["mu"],
                                           start=True, stop=True))
            p.emit("t", lambda e: e.matmul(rot(sl_d), onesrow_f[:], ROWS["rstd"],
                                           start=True, stop=True).then_inc(s_mm, 1))
            c_mm.inc()
            p.wait("v", s_mm, c_mm.v)
            p.vstep(lambda e: e.tensor_copy(mub[:], rot(sl_c)))
            ve_bc = p.vstep(lambda e: e.tensor_copy(rb[:], rot(sl_d)))
            slot_free[sl_c] = slot_free[sl_d] = (s_ve, ve_bc)
            norm_at = {}
            for dc in range(DC):
                b4 = dc % 4

                def ldx2(e, dc=dc, b4=b4):
                    e.wait_ge(s_ve, ve_bc if dc < 4 else norm_at[dc - 4])
                    e.dma_start(out=xch[:, b4 * T:(b4 + 1) * T],
                                in_=src[dc][:, tile * T:(tile + 1) * T]).then_inc(SC[b4], 16)
                p.emit("y", ldx2)
                cSC[b4].inc(16)
                p.wait("v", SC[b4], cSC[b4].v)
                p.vstep(lambda e, b4=b4: e.tensor_tensor(
                    x2ch[:, 0:T], xch[:, b4 * T:(b4 + 1) * T], mub[:], op=ALU.subtract))
                norm_at[dc] = p.vstep(lambda e, dc=dc, xb=xb: e.tensor_tensor(
                    xnv(xb)[:, dc * T:(dc + 1) * T], x2ch[:, 0:T], rb[:], op=ALU.mult))
            return norm_at[DC - 1]

        def gating_tile(ve_xn, xb):
            mm_now, ve_now = c_mm.v, c_ve.v
            gate_done, logtt, wb_at = {}, {}, {}
            for tc in range(4):
                sl = take_slot()
                p.wait("t", s_ve, ve_xn if tc == 0 else logtt[tc - 1])
                for dc in range(DC):
                    p.emit("t", lambda e, dc=dc, tc=tc, sl=sl, xb=xb: e.matmul(
                        rot(sl)[:, 0:7], xnv(xb)[:, dc * T + tc * 128: dc * T + (tc + 1) * 128],
                        gwsb[:, dc * 7:(dc + 1) * 7], start=(dc == 0), stop=(dc == DC - 1)))
                p.emit("t", lambda e, sl=sl, xb=xb: e.matmul(
                    rot(sl)[:, 8:512], xnv(xb)[:, 0:128], xnv(xb)[:, 0:504],
                    start=False, stop=True).then_inc(s_mm, 1))
                c_mm.inc()
                p.wait("v", s_mm, c_mm.v)
                p.vstep(lambda e, sl=sl: e.tensor_tensor(GT["log"], rot(sl)[:, 0:7],
                                                         gbc[:, 0:7], op=ALU.add))
                logtt[tc] = c_ve.v
                slot_free[sl] = (s_ve, logtt[tc])
                p.vstep(lambda e: e.reduce_max(COL["m1"], GT["log"], axis=AXX))
                p.vstep(lambda e: e.tensor_scalar(GT["pp"], GT["log"], COL["m1"], None,
                                                  op0=ALU.subtract))
                p.wait("s", s_ve, c_ve.v)
                p.emit("s", lambda e: e.activation(GT["pp"], GT["pp"], AF.Exp,
                                                   accum_out=COL["sum"]).then_inc(s_sc, 1))
                c_sc.inc()
                p.wait("v", s_sc, c_sc.v)
                for f in [
                    lambda e: e.reciprocal(COL["r"], COL["sum"]),
                    lambda e: e.tensor_scalar(GT["pp"], GT["pp"], COL["r"], None, op0=ALU.mult),
                    lambda e: e.tensor_tensor(GT["b"], GT["pp"], bbc[:, 0:7], op=ALU.add),
                    lambda e: e.reduce_max(COL["m1"], GT["b"], axis=AXX),
                    lambda e: e.tensor_scalar(GT["eqm"], GT["b"], COL["m1"], NEG,
                                              op0=ALU.is_ge, op1=ALU.mult),
                    lambda e: e.tensor_tensor(GT["b2"], GT["b"], GT["eqm"], op=ALU.add),
                    lambda e: e.reduce_max(COL["m2"], GT["b2"], axis=AXX),
                    lambda e: e.tensor_scalar(GT["eqm"], GT["b2"], COL["m2"], NEG,
                                              op0=ALU.is_ge, op1=ALU.mult),
                    lambda e: e.tensor_tensor(GT["b2"], GT["b2"], GT["eqm"], op=ALU.add),
                    lambda e: e.reduce_max(COL["m3"], GT["b2"], axis=AXX),
                    lambda e: e.tensor_scalar(GT["mask"], GT["b"], COL["m3"], None, op0=ALU.is_ge),
                    lambda e: e.tensor_tensor(GT["pg"], GT["pp"], GT["mask"], op=ALU.mult),
                    lambda e: e.reduce_sum(COL["sum"], GT["pg"], axis=AXX),
                    lambda e: e.reciprocal(COL["r"], COL["sum"]),
                ]:
                    p.vstep(f)
                ve_w = p.vstep(lambda e: e.tensor_scalar(GT["w05"], GT["pg"], COL["r"], 0.5,
                                                         op0=ALU.mult, op1=ALU.mult))
                p.wait("t", s_ve, ve_w)
                p.emit("t", lambda e: e.transpose(ps_tr[0:7, 0:128], GT["w05"],
                                                  ident[:]).then_inc(s_mm, 1))
                c_mm.inc()
                p.wait("v", s_mm, c_mm.v)
                gate_done[tc] = p.vstep(lambda e, tc=tc: e.tensor_copy(
                    wTsb[:, tc * 128:(tc + 1) * 128], ps_tr[0:7, 0:128]))
            ve_gate = gate_done[3]
            for ex in range(7):
                def rowdma(e, ex=ex):
                    if ex == 0:
                        e.wait_ge(s_ve, ve_gate)
                        e.wait_ge(s_mm, mm_now)
                    e.dma_start(out=wrow[0:1, ex * T:(ex + 1) * T],
                                in_=wTsb[ex:ex + 1, :]).then_inc(s_row, 16)
                p.emit("g", rowdma)
                c_row.inc(16)
            p.wait("t", s_row, c_row.v)
            p.wait("t", s_ve, ve_gate)
            for ex in range(7):
                sl = take_slot()
                p.emit("t", lambda e, ex=ex, sl=sl: e.matmul(
                    rot(sl), onesrow_b[:], wrow[0:1, ex * T:(ex + 1) * T],
                    start=True, stop=True).then_inc(s_mm, 1))
                c_mm.inc()
                p.wait("s", s_mm, c_mm.v)
                if ex == 0:
                    p.wait("s", s_ve, ve_now)
                p.emit("s", lambda e, ex=ex, sl=sl, xb=xb: e.copy(
                    wbtv(xb)[:, (ex + 1) * T:(ex + 2) * T], rot(sl)).then_inc(s_sc, 1))
                c_sc.inc()
                wb_at[ex] = c_sc.v
                slot_free[sl] = (s_sc, c_sc.v)
            return wb_at

        def ffn_tile(call, tile, ve_xn, wb_at, prev_acc_reader, xb, hooks=None):
            yev = {}
            first_pass = (call == 0 and tile == 0)
            g0 = call * NT * GPT + tile * GPT

            mm_before = c_mm.v if (call == 1 and tile == 0) else None

            def emit_prefetch_range(k0, k1):
              for k in range(k0, k1):
                gg = g0 + k

                b = gg % NBUF
                tee_barrier = cSTE[:] if (mm_before is not None and k == 0) else None
                tee_barrier_v = [c.v for c in cSTE] if tee_barrier else None

                def pf(e, gg=gg, k=k, b=b, tee_barrier_v=tee_barrier_v):
                    if mm_before is not None and k == 0:
                        e.wait_ge(s_mm, mm_before)
                        for bi in range(NBUF):
                            if tee_barrier_v[bi] > 0:
                                e.wait_ge(STE[bi], tee_barrier_v[bi])
                    if gg >= DEPTH:
                        e.wait_ge(s_mm, mm_of_group[gg - DEPTH])
                    src_blk = wsw[gg % GPT] if first_pass else wbf[gg % GPT]
                    kw = dict(max_dma_last_dim=4096) if first_pass else {}
                    if first_pass and gg >= NBUF:
                        e.wait_ge(STE[b], tee_tgt[gg - NBUF])
                    if (not first_pass) and call == 0:
                        e.wait_ge(STE[b], tee_tgt[gg % GPT])
                    e.dma_start(out=wstream[:, b * 2048:(b + 1) * 2048],
                                in_=src_blk, **kw).then_inc(SW[b], 16)
                p.emit("g", pf)
                cSW[b].inc(16)
                wld_tgt[gg] = cSW[b].v
                if first_pass:
                    def tee(e, gg=gg, b=b, wt=wld_tgt[gg]):
                        e.wait_ge(SW[b], wt)
                        e.dma_start(out=wbf[gg],
                                    in_=wstream[:, b * 2048:(b + 1) * 2048]
                                    ).then_inc(STE[b], 16)
                    p.emit("y", tee)
                    cSTE[b].inc(16)
                    tee_tgt[gg] = cSTE[b].v

            def mm_group(gg, rhs_fn, extra=()):
                sl = take_slot()
                p.wait("t", SW[gg % NBUF], wld_tgt[gg])
                for w in extra:
                    p.wait("t", *w)
                buf = gg % NBUF
                for k in range(DC):
                    def mmf(e, sl=sl, buf=buf, k=k, rhs_fn=rhs_fn):
                        ins = e.matmul(rot(sl),
                                       wstream[:, buf * 2048 + k * 128: buf * 2048 + (k + 1) * 128],
                                       rhs_fn(k), start=(k == 0), stop=(k == DC - 1))
                        if k == DC - 1:
                            ins.then_inc(s_mm, 1)
                    p.emit("t", mmf)
                c_mm.inc()
                mm_of_group[gg] = c_mm.v
                return sl

            gg = g0
            for e_ in range(E):
                if hooks and e_ in hooks:
                    hooks[e_]()
                emit_prefetch_range(e_ * 48, (e_ + 1) * 48)
                for hc in range(DC):
                    sl = mm_group(gg, lambda k, xb=xb: xnv(xb)[:, k * T:(k + 1) * T])
                    p.wait("s", s_mm, c_mm.v)
                    if t1_of_par.get(hc % 2):
                        p.wait("s", s_ve, t1_of_par[hc % 2])
                    p.emit("s", lambda e, sl=sl, hc=hc: e.activation(
                        sch[:, (hc % 2) * T:(hc % 2 + 1) * T], rot(sl),
                        AF.Silu).then_inc(s_sc, 1))
                    c_sc.inc()
                    slot_free[sl] = (s_sc, c_sc.v)
                    sc_silu = c_sc.v
                    gg += 1
                    sl = mm_group(gg, lambda k, xb=xb: xnv(xb)[:, k * T:(k + 1) * T])
                    p.wait("v", s_mm, c_mm.v)
                    p.wait("v", s_sc, sc_silu)
                    tgt_t1 = p.vstep(lambda e, sl=sl, hc=hc: e.tensor_tensor(
                        t1ch[:, (hc % 2) * T:(hc % 2 + 1) * T], rot(sl),
                        sch[:, (hc % 2) * T:(hc % 2 + 1) * T], op=ALU.mult))
                    slot_free[sl] = (s_ve, tgt_t1)
                    t1_of_par[hc % 2] = tgt_t1
                    if e_ > 0 and hc == 0:
                        p.wait("v", s_sc, wb_at[e_ - 1])
                    if first_pass and e_ == 0 and hc == 0:
                        p.wait("v", s_ve, ve_wbt0)
                    p.vstep(lambda e, hc=hc, e_=e_, xb=xb: e.tensor_tensor(
                        hp[:, hc * T:(hc + 1) * T], t1ch[:, (hc % 2) * T:(hc % 2 + 1) * T],
                        wbtv(xb)[:, e_ * T:(e_ + 1) * T], op=ALU.mult))
                    gg += 1
                hp_ready = c_ve.v
                for dc in range(DC):
                    extra = [(s_ve, hp_ready)] if dc == 0 else []
                    if e_ == 0 and dc == 0 and prev_acc_reader is not None:
                        extra.append(prev_acc_reader)
                    sl = mm_group(gg, lambda k: hp[:, k * T:(k + 1) * T], extra)
                    p.wait("v", s_mm, c_mm.v)
                    if e_ == 0:
                        tgt = p.vstep(lambda e, sl=sl, dc=dc: e.tensor_copy(
                            acc[:, dc * T:(dc + 1) * T], rot(sl)))
                    else:
                        tgt = p.vstep(lambda e, sl=sl, dc=dc: e.tensor_tensor(
                            acc[:, dc * T:(dc + 1) * T], rot(sl), acc[:, dc * T:(dc + 1) * T],
                            op=ALU.add))
                    slot_free[sl] = (s_ve, tgt)
                    if e_ == E - 1:
                        yev[dc] = tgt
                    gg += 1
                if debug and call == 1 and tile == 1:
                    for dc in range(DC):
                        def dste(e, e_=e_, dc=dc, tgt=tgt):
                            if dc == 0:
                                e.wait_ge(s_ve, tgt)
                            e.dma_start(out=dbg_e[e_][dc],
                                        in_=acc[:, dc * T:(dc + 1) * T]).then_inc(s_dbg, 16)
                        p.emit("y", dste)
                        c_dbg.inc(16)
            return yev

        def residual_store_tile(tile, src, dst, yev):
            racc_tgt = {}
            for dc in range(DC):
                def rac(e, dc=dc, tgt=yev[dc]):
                    e.wait_ge(s_ve, tgt)
                    e.dma_start(out=acc[:, dc * T:(dc + 1) * T],
                                in_=src[dc][:, tile * T:(tile + 1) * T],
                                accum_op=ALU.add).then_inc(s_racc, 16)
                p.emit("g", rac)
                c_racc.inc(16)
                racc_tgt[dc] = c_racc.v

                def stx(e, dc=dc, tile=tile, tgt=racc_tgt[dc]):
                    e.wait_ge(s_racc, tgt)
                    e.dma_start(out=dst[dc][:, tile * T:(tile + 1) * T],
                                in_=acc[:, dc * T:(dc + 1) * T]).then_inc(s_acst, 16)
                p.emit("y", stx)
                c_acst.inc(16)
            return (s_acst, c_acst.v)

        exit_state = {"prev_ev": None, "st": {}}

        def residual_exit_tile(tile, src, yev):
            racc_tgt = {}
            for dc in range(DC):
                def rac(e, dc=dc, tgt=yev[dc]):
                    e.wait_ge(s_ve, tgt)
                    e.dma_start(out=acc[:, dc * T:(dc + 1) * T],
                                in_=src[dc][:, tile * T:(tile + 1) * T],
                                accum_op=ALU.add).then_inc(s_racc, 16)
                p.emit("g", rac)
                c_racc.inc(16)
                racc_tgt[dc] = c_racc.v
            p.wait("t", s_racc, racc_tgt[DC - 1])
            if debug:
                for dc in range(DC):
                    def dstx(e, dc=dc, tile=tile, tgt=res_at[dc]):
                        e.wait_ge(s_ve, tgt)
                        e.dma_start(out=dbg_acc[dc][:, tile * T:(tile + 1) * T],
                                    in_=acc[:, dc * T:(dc + 1) * T]).then_inc(s_st, 16)
                    p.emit("y", dstx)
                    c_st.inc(16)
            for tc in range(4):
                ob = tc % 2
                ev_at = {}
                for dcg in range(4):
                    prev = ev_at.get(dcg - 1, exit_state["prev_ev"])
                    if prev is not None:
                        p.wait("t", s_ve, prev)
                    for j in range(4):
                        dc = dcg * 4 + j

                        def trf(e, dc=dc, j=j, tc=tc, fin=(j == 3)):
                            ins = e.transpose(
                                ps_tr[:, j * 128:(j + 1) * 128],
                                acc[:, dc * T + tc * 128: dc * T + (tc + 1) * 128], ident[:])
                            if fin:
                                ins.then_inc(s_mm, 1)
                        p.emit("t", trf)
                    c_mm.inc()
                    p.wait("v", s_mm, c_mm.v)
                    key = (tile, tc - 2) if tc >= 2 else (tile - 1, tc + 2)
                    if dcg == 0 and key in exit_state["st"]:
                        p.wait("v", SY[ob], exit_state["st"][key])
                    ev_at[dcg] = p.vstep(lambda e, ob=ob, dcg=dcg: e.tensor_copy(
                        otb[:, ob * 2048 + dcg * 512: ob * 2048 + (dcg + 1) * 512], ps_tr[:]))
                exit_state["prev_ev"] = ev_at[3]

                def sty(e, tile=tile, tc=tc, ob=ob, tgt=ev_at[3]):
                    e.wait_ge(s_ve, tgt)
                    e.dma_start(out=y_out[tile * T + tc * 128: tile * T + (tc + 1) * 128, :],
                                in_=otb[:, ob * 2048:(ob + 1) * 2048]).then_inc(SY[ob], 16)
                p.emit("y", sty)
                cSY[ob].inc(16)
                exit_state["st"][(tile, tc)] = cSY[ob].v
            return (s_mm, c_mm.v)

        def attention(src, dst, tail_hook=None):
            qkv_done, pv_ev, wo_st = {}, {}, {}
            ve_xn_of = {}

            def attn_ln_qkv(sl_):
                p.mark(f"attn.ln{sl_}")
                ve_xn_of[sl_] = ln_tile(src, sl_, sl_ % 2)
                for m in range(3):
                    sl = take_slot()
                    if m == 0:
                        p.wait("t", s_ve, ve_xn_of[sl_])
                    for dc in range(DC):
                        def mmq(e, sl=sl, m=m, dc=dc, sl_=sl_):
                            ins = e.matmul(rot(sl),
                                           wqkv[:, m * D + dc * 128: m * D + (dc + 1) * 128],
                                           xnv(sl_ % 2)[:, dc * T:(dc + 1) * T],
                                           start=(dc == 0), stop=(dc == DC - 1))
                            if dc == DC - 1:
                                ins.then_inc(s_mm, 1)
                        p.emit("t", mmq)
                    c_mm.inc()
                    p.wait("v", s_mm, c_mm.v)
                    if m == 0:
                        tgt = p.vstep(lambda e, sl=sl, sl_=sl_: e.tensor_scalar(
                            qsb[:, sl_ * T:(sl_ + 1) * T], rot(sl), bqkv[:, 0:1], ISQ,
                            op0=ALU.add, op1=ALU.mult))
                    elif m == 1:
                        tgt = p.vstep(lambda e, sl=sl, sl_=sl_: e.tensor_scalar(
                            ksb[:, sl_ * T:(sl_ + 1) * T], rot(sl), bqkv[:, 1:2], None, op0=ALU.add))
                    else:
                        tgt = p.vstep(lambda e, sl=sl, sl_=sl_: e.tensor_scalar(
                            sch[:, 0:T], rot(sl), bqkv[:, 2:3], None, op0=ALU.add))
                    slot_free[sl] = (s_ve, tgt)
                p.wait("t", s_ve, c_ve.v)
                for j in range(4):
                    def trv(e, j=j, fin=(j == 3)):
                        ins = e.transpose(ps_trb[:, j * 128:(j + 1) * 128],
                                          sch[:, j * 128:(j + 1) * 128], identb[:])
                        if fin:
                            ins.then_inc(s_mm, 1)
                    p.emit("t", trv)
                c_mm.inc()
                p.wait("v", s_mm, c_mm.v)
                qkv_done[sl_] = p.vstep(lambda e, sl_=sl_: e.tensor_copy(
                    vT[:, sl_ * T:(sl_ + 1) * T], ps_trb[:, 0:512]))

            trB_done = {}

            def attn_rows_A(s_, qi):
                half = qi % 2
                ab = half * 3
                sneg = acc[:, half * 4096: half * 4096 + 4096]
                nks = s_ + 1
                qc = s_ * 4 + qi
                last_tgt = None
                first_ts = True
                for ks in range(nks):
                    sl = take_slot()
                    p.emit("t", lambda e, sl=sl, qc=qc, ks=ks: e.matmul(
                        rot(sl), qsb[:, qc * 128:(qc + 1) * 128],
                        ksb[:, ks * T:(ks + 1) * T], start=True,
                        stop=True).then_inc(s_mm, 1))
                    c_mm.inc()
                    p.wait("v", s_mm, c_mm.v)
                    if first_ts and trB_done.get(half) is not None:
                        p.wait("v", s_mm, trB_done[half])
                    first_ts = False
                    first = ks == 0
                    if ks < nks - 1:
                        last_tgt = p.vstep(lambda e, sl=sl, ks=ks, first=first, sneg=sneg, ab=ab:
                                           e.tensor_scalar(
                            sneg[:, ks * T:(ks + 1) * T], rot(sl), -1.0,
                            None if first else arow[:, ab:ab + 1],
                            op0=ALU.mult, op1=ALU.min, accum_out=arow[:, ab:ab + 1]))
                    else:
                        p.vstep(lambda e, sl=sl, qi=qi: e.tensor_tensor(
                            x2ch[:, 0:T], rot(sl), masks[:, qi * T:(qi + 1) * T], op=ALU.add))
                        last_tgt = p.vstep(lambda e, ks=ks, first=first, sneg=sneg, ab=ab:
                                           e.tensor_scalar(
                            sneg[:, ks * T:(ks + 1) * T], x2ch[:, 0:T], -1.0,
                            None if first else arow[:, ab:ab + 1],
                            op0=ALU.mult, op1=ALU.min, accum_out=arow[:, ab:ab + 1]))
                    slot_free[sl] = (s_ve, last_tgt)
                p.wait("s", s_ve, last_tgt)
                p.emit("s", lambda e, nks=nks, sneg=sneg, ab=ab: e.activation(
                    sneg[:, 0:nks * T], sneg[:, 0:nks * T], AF.Exp, scale=-1.0,
                    bias=arow[:, ab:ab + 1], accum_out=arow[:, ab + 1:ab + 2]).then_inc(s_sc, 1))
                c_sc.inc()
                p.wait("v", s_sc, c_sc.v)
                p.vstep(lambda e, ab=ab: e.reciprocal(arow[:, ab + 2:ab + 3], arow[:, ab + 1:ab + 2]))
                ve_p = p.vstep(lambda e, nks=nks, sneg=sneg, ab=ab: e.tensor_scalar(
                    sneg[:, 0:nks * T], sneg[:, 0:nks * T], arow[:, ab + 2:ab + 3],
                    None, op0=ALU.mult))
                return ve_p

            def attn_rows_B(s_, qi, ve_p):
                half = qi % 2
                sneg = acc[:, half * 4096: half * 4096 + 4096]
                nks = s_ + 1
                nkc = nks * 4
                p.wait("t", s_ve, ve_p)
                ptr_ev = {}
                for kc in range(nkc):
                    jj = kc % 4
                    if jj == 0 and kc >= 4:
                        p.wait("t", s_ve, ptr_ev[kc - 4])
                    fin = (jj == 3 or kc == nkc - 1)

                    def trp(e, kc=kc, jj=jj, fin=fin, sneg=sneg):
                        ins = e.transpose(ps_tr[:, jj * 128:(jj + 1) * 128],
                                          sneg[:, kc * 128:(kc + 1) * 128], ident[:])
                        if fin:
                            ins.then_inc(s_mm, 1)
                    p.emit("t", trp)
                    if fin:
                        c_mm.inc()
                        p.wait("v", s_mm, c_mm.v)
                        lo = (kc // 4) * 4
                        width = (kc - lo + 1) * 128
                        ptr_ev[lo] = p.vstep(lambda e, lo=lo, width=width: e.tensor_copy(
                            hp[:, lo * 128: lo * 128 + width], ps_tr[:, 0:width]))
                trB_done[half] = c_mm.v
                ve_ptb = ptr_ev[(nkc - 1) // 4 * 4]
                pv_ps = ps_wa if qi % 2 == 0 else ps_wb
                p.wait("t", s_ve, ve_ptb)
                if pv_ev.get(qi % 2):
                    p.wait("t", s_ve, pv_ev[qi % 2])
                p.wait("t", s_ve, qkv_done[s_])
                for kc in range(nkc):
                    def mpv(e, kc=kc, pv_ps=pv_ps, nkc=nkc):
                        ins = e.matmul(pv_ps[:, 0:128], vT[:, kc * 128:(kc + 1) * 128],
                                       hp[:, kc * 128:(kc + 1) * 128],
                                       start=(kc == 0), stop=(kc == nkc - 1))
                        if kc == nkc - 1:
                            ins.then_inc(s_mm, 1)
                    p.emit("t", mpv)
                c_mm.inc()
                trB_done[half] = c_mm.v
                p.wait("v", s_mm, c_mm.v)
                pv_ev[qi % 2] = p.vstep(lambda e, pv_ps=pv_ps, qi=qi: e.tensor_copy(
                    sch[:, T + qi * 128: T + (qi + 1) * 128], pv_ps[:, 0:128]))

            def attn_rows_wo(s_):
                p.mark(f"attn.rows{s_}")
                p.wait("v", SX2[0], cSX2[0].v)
                p.wait("v", SX2[1], cSX2[1].v)
                vp = {}
                vp[0] = attn_rows_A(s_, 0)
                vp[1] = attn_rows_A(s_, 1)
                attn_rows_B(s_, 0, vp[0])
                vp[2] = attn_rows_A(s_, 2)
                attn_rows_B(s_, 1, vp[1])
                vp[3] = attn_rows_A(s_, 3)
                attn_rows_B(s_, 2, vp[2])
                attn_rows_B(s_, 3, vp[3])
                attnF = sch[:, T:2 * T]
                p.mark(f"attn.wo{s_}")
                for dc in range(DC):
                    sl = take_slot()
                    if dc == 0:
                        p.wait("t", s_ve, pv_ev[1])
                    p.emit("t", lambda e, sl=sl, dc=dc: e.matmul(
                        rot(sl), wo[:, dc * 128:(dc + 1) * 128], attnF,
                        start=True, stop=True).then_inc(s_mm, 1))
                    c_mm.inc()
                    p.wait("v", s_mm, c_mm.v)
                    if dc >= 2:
                        p.wait("v", SX2[dc % 2], wo_st[dc - 2])
                    elif wo_st.get(dc + 14):
                        p.wait("v", SX2[dc % 2], wo_st[dc + 14])
                    tgt = p.vstep(lambda e, sl=sl, dc=dc: e.tensor_scalar(
                        x2ch[:, (dc % 2) * T:(dc % 2 + 1) * T], rot(sl), bo[:, dc:dc + 1],
                        None, op0=ALU.add))
                    slot_free[sl] = (s_ve, tgt)

                    def rac(e, dc=dc, s=s_, tgt=tgt):
                        e.wait_ge(s_ve, tgt)
                        e.dma_start(out=x2ch[:, (dc % 2) * T:(dc % 2 + 1) * T],
                                    in_=src[dc][:, s * T:(s + 1) * T],
                                    accum_op=ALU.add).then_inc(s_racc, 16)
                    p.emit("g", rac)
                    c_racc.inc(16)

                    def sth(e, dc=dc, s=s_, tgt=c_racc.v):
                        e.wait_ge(s_racc, tgt)
                        e.dma_start(out=dst[dc][:, s * T:(s + 1) * T],
                                    in_=x2ch[:, (dc % 2) * T:(dc % 2 + 1) * T]).then_inc(SX2[dc % 2], 16)
                    p.emit("y", sth)
                    cSX2[dc % 2].inc(16)
                    wo_st[dc] = cSX2[dc % 2].v

            attn_ln_qkv(0)
            for s_ in range(NT):
                if s_ + 1 < NT:
                    attn_ln_qkv(s_ + 1)
                if s_ == NT - 1 and tail_hook is not None:
                    tail_hook()
                attn_rows_wo(s_)

        XF = [xf[dc] for dc in range(DC)]
        H1 = [h1[dc] for dc in range(DC)]
        H2 = [h2[dc] for dc in range(DC)]

        def run_ffn_call(call, src, dst, prev, pre=None):
            nxt = {}
            if pre is None:
                p.mark(f"c{call}.ln0")
                ve_xn = ln_tile(src, 0, 0)
                wb_at = gating_tile(ve_xn, 0)
            else:
                ve_xn, wb_at = pre
            for tile in range(NT):
                xb = tile % 2
                first_pass = (call == 0 and tile == 0)
                hooks = None
                if tile + 1 < NT and not first_pass:
                    nb = (tile + 1) % 2
                    def mk_ln(tile=tile, nb=nb):
                        p.mark(f"c{call}.pln{tile + 1}")
                        nxt["ve"] = ln_tile(src, tile + 1, nb)
                    def mk_gate(nb=nb):
                        nxt["wb"] = gating_tile(nxt["ve"], nb)
                    hooks = {2: mk_ln, 4: mk_gate}
                p.mark(f"c{call}.exp{tile}")
                yev = ffn_tile(call, tile, ve_xn, wb_at, prev, xb, hooks)
                p.mark(f"c{call}.res{tile}")
                if dst is not None:
                    prev = residual_store_tile(tile, src, dst, yev)
                else:
                    prev = residual_exit_tile(tile, src, yev)
                if tile + 1 < NT:
                    if first_pass:
                        nxt["ve"] = ln_tile(src, tile + 1, (tile + 1) % 2)
                        nxt["wb"] = gating_tile(nxt["ve"], (tile + 1) % 2)
                    ve_xn, wb_at = nxt["ve"], nxt["wb"]
            return prev

        run_ffn_call(0, XF, H1, None)
        p.mark("attn")
        f2pre = {}

        def tail():
            p.mark("c1.ln0")
            ve0 = ln_tile(H2, 0, 0)
            f2pre["pre"] = (ve0, gating_tile(ve0, 0))
        attention(H1, H2, tail_hook=tail)
        run_ffn_call(1, H2, None, (s_sc, c_sc.v), pre=f2pre["pre"])

        if debug:
            def dmp(e):
                e.dma_start(out=dbg_xf, in_=xf).then_inc(s_dbg, 16)
                e.dma_start(out=dbg_h1, in_=h1).then_inc(s_dbg, 16)
                e.dma_start(out=dbg_h2, in_=h2).then_inc(s_dbg, 16)
            p.emit("y", dmp)
            c_dbg.inc(48)
            p.wait("y", s_dbg, c_dbg.v)
        p.wait("y", SY[0], cSY[0].v)
        p.wait("y", SY[1], cSY[1].v)
        p.wait("y", s_acst, c_acst.v)

        semnames = {id(s_): n for s_, n in
                    [(s_row, "row"), (s_mm, "mm"), (s_sc, "sc"), (s_ve, "ve"),
                     (s_cst, "cst"), (s_acst, "acst"), (s_dbg, "dbg"), (s_racc, "racc")] +
                    [(SC[i], f"xc{i}") for i in range(4)] +
                    [(SW[i], f"wl{i}") for i in range(NBUF)] +
                    [(STE[i], f"te{i}") for i in range(NBUF)] +
                    [(SO[i], f"ot{i}") for i in range(2)] +
                    [(SY[i], f"yo{i}") for i in range(2)] +
                    [(SX2[i], f"x2{i}") for i in range(2)]}
        simulate(p, semnames)

        with nc.Block() as block:
            @block.tensor
            def _(t):
                for f in p.q["t"]:
                    f(t)

            @block.vector
            def _(v):
                for f in p.q["v"]:
                    f(v)

            @block.scalar
            def _(s):
                for f in p.q["s"]:
                    f(s)

            @block.sync
            def _(y):
                for f in p.q["y"]:
                    f(y)

            @block.gpsimd
            def _(g):
                for f in p.q["g"]:
                    f(g)
    return nc


def _swz(w):
    return np.ascontiguousarray(
        np.asarray(w, np.float32).reshape(16, 128, 16, 128).transpose(2, 1, 0, 3)
    ).reshape(16, 128, 2048)


def host_stage(inputs):
    f = lambda k: np.asarray(inputs[k], np.float32)
    W1, Wg, W2 = f("L0_ffn_W1"), f("L0_ffn_Wg"), f("L0_ffn_W2")
    blocks = []
    for e_ in range(E):
        if e_ == 0:
            a, b, c = f("L0_ffn_Ws1"), f("L0_ffn_Wsg"), f("L0_ffn_Ws2")
        else:
            a, b, c = W1[e_ - 1], Wg[e_ - 1], W2[e_ - 1]
        asw, bsw, csw = _swz(a), _swz(b), _swz(c)
        for hc in range(16):
            blocks.append(asw[hc])
            blocks.append(bsw[hc])
        for dc in range(16):
            blocks.append(csw[dc])
    wsw = np.ascontiguousarray(np.stack(blocks))

    gate_w = f("L0_ffn_gate_w")
    gw = np.ascontiguousarray(gate_w.reshape(16, 128, 7).transpose(1, 0, 2).reshape(128, 112))
    pad8 = lambda v: np.tile(
        np.concatenate([np.asarray(v, np.float32), [0.0]]).astype(np.float32), (128, 1))
    masks = np.zeros((4, 128, T), np.float32)
    q_idx = np.arange(128)[:, None]
    k_idx = np.arange(T)[None, :]
    for r in range(4):
        masks[r] = np.where(k_idx <= r * 128 + q_idx, 0.0, NEG).astype(np.float32)
    masks = np.ascontiguousarray(masks.transpose(1, 0, 2).reshape(128, 4 * T))

    common = dict(
        wsw=wsw,
        gw_in=gw,
        gbb_in=pad8(f("L0_ffn_gate_b")),
        bbb_in=pad8(f("L0_ffn_biases")),
        wq_in=np.ascontiguousarray(f("L0_attn_Wq").reshape(DC, 128, HD).transpose(1, 0, 2).reshape(128, DC * HD)),
        wk_in=np.ascontiguousarray(f("L0_attn_Wk").reshape(DC, 128, HD).transpose(1, 0, 2).reshape(128, DC * HD)),
        wv_in=np.ascontiguousarray(f("L0_attn_Wv").reshape(DC, 128, HD).transpose(1, 0, 2).reshape(128, DC * HD)),
        wo_in=np.ascontiguousarray(f("L0_attn_Wo")),
        bqkv_in=np.ascontiguousarray(
            np.stack([f("L0_attn_bq"), f("L0_attn_bk"), f("L0_attn_bv"),
                      np.zeros(128, np.float32)], 1)),
        bo_in=np.ascontiguousarray(f("L0_attn_bo").reshape(DC, 128).T),
        ones_in=np.ones((128, 128), np.float32),
        onescol_in=np.ones((128, 2), np.float32),
        ident_in=np.eye(128, dtype=np.float32),
        masks_in=masks,
    )
    x = np.asarray(inputs["L0_attn_x"], np.float32)
    in_maps = []
    for c in range(8):
        m = dict(common)
        m["x_in"] = np.ascontiguousarray(x[c])
        in_maps.append(m)
    return in_maps


def kernel(**inputs):
    in_maps = host_stage(inputs)
    nc = build_nc()
    res = run_bass_kernel_spmd(nc, in_maps, core_ids=list(range(8)))
    return np.stack([res.results[c]["y_out"] for c in range(8)]).astype(np.float32)


if __name__ == "__main__":
    nc = build_nc()
    print("build ok, n_inst =", sum(1 for _ in nc.all_instructions()))


# version 07
# speedup vs baseline: 1.0071x; 1.0071x over previous
import contextlib
import numpy as np
import concourse.bass as bass
import concourse.mybir as mybir
from concourse.bass_utils import run_bass_kernel_spmd

F32 = mybir.dt.float32
BF16 = mybir.dt.bfloat16
AF = mybir.ActivationFunctionType
ALU = mybir.AluOpType
AXX = mybir.AxisListType.X

D, DC, L, T, NT, E, HD = 2048, 16, 4096, 512, 8, 8, 128
NBUF, DEPTH = 6, 5
GPT = E * 48
NEG = -1.0e30
ISQ = float(1.0 / np.sqrt(128.0))


class Ctr:
    def __init__(self):
        self.v = 0

    def inc(self, n=1):
        self.v += n
        return self.v


class Prog:
    def __init__(self):
        self.q = {k: [] for k in ("t", "v", "s", "y", "g")}
        self.vsem = None
        self.vctr = None
        self.marks = []
        self.cur = "init"
        self.tags = {k: [] for k in ("t", "v", "s", "y", "g")}

    def emit(self, eng, f):
        self.q[eng].append(f)
        self.tags[eng].append(self.cur)

    def wait(self, eng, sem, val):
        if val > 0:
            self.q[eng].append(lambda e, sem=sem, val=val: e.wait_ge(sem, val))
            self.tags[eng].append(self.cur + "|W")

    def mark(self, label):
        self.cur = label
        self.marks.append((label, {k: len(v) for k, v in self.q.items()}))

    def vstep(self, f):
        sem = self.vsem
        self.q["v"].append(lambda e, f=f, sem=sem: f(e).then_inc(sem, 1))
        self.tags["v"].append(self.cur + "|VS")
        tgt = self.vctr.inc()
        self.q["v"].append(lambda e, sem=sem, tgt=tgt: e.wait_ge(sem, tgt))
        self.tags["v"].append(self.cur + "|VSW")
        return tgt


class _MockIns:
    def __init__(self, sim):
        self.sim = sim

    def then_inc(self, sem, n=1):
        self.sim.pending.append((id(sem), n))
        return self


class _MockEng:
    def __init__(self, sim):
        self.sim = sim

    def wait_ge(self, sem, val):
        self.sim.waits.append((id(sem), val))
        return _MockIns(self.sim)

    def __getattr__(self, name):
        return lambda *a, **k: _MockIns(self.sim)


def simulate(p, semnames):
    class S:
        pending, waits = [], []
    sim = S()
    sems, ptrs = {}, {k: 0 for k in p.q}
    engs = {k: _MockEng(sim) for k in p.q}
    progress = True
    while progress:
        progress = False
        for k in p.q:
            while ptrs[k] < len(p.q[k]):
                sim.waits, sim.pending = [], []
                p.q[k][ptrs[k]](engs[k])
                if not all(sems.get(s_, 0) >= v for s_, v in sim.waits):
                    break
                for s_, n in sim.pending:
                    sems[s_] = sems.get(s_, 0) + n
                ptrs[k] += 1
                progress = True
    stuck = {k: (ptrs[k], len(p.q[k])) for k in p.q if ptrs[k] < len(p.q[k])}
    if stuck:
        for k in stuck:
            around = [p.tags[k][i] if 0 <= i < len(p.tags[k]) else "?"
                      for i in range(ptrs[k] - 2, ptrs[k] + 3)]
            print(f"stuck {k}@{ptrs[k]}: tags around {around}")
        det = {}
        for k in stuck:
            sim.waits, sim.pending = [], []
            p.q[k][ptrs[k]](engs[k])
            det[k] = [(semnames.get(s_, "?"), v, sems.get(s_, 0)) for s_, v in sim.waits]
        raise RuntimeError(f"program deadlock: {stuck} {det}")


def build_nc(debug=False):
    nc = bass.Bass("TRN2", target_bir_lowering=False)
    dr = lambda n, shp, dt=F32, kind="ExternalInput": nc.dram_tensor(n, shp, dt, kind=kind).ap()
    x_in = dr("x_in", [L, D])
    wsw = dr("wsw", [GPT, 128, 2048])
    gw_in = dr("gw_in", [128, DC * 7])
    gbb_in = dr("gbb_in", [128, 8])
    bbb_in = dr("bbb_in", [128, 8])
    wq_in = dr("wq_in", [128, DC * HD])
    wk_in = dr("wk_in", [128, DC * HD])
    wv_in = dr("wv_in", [128, DC * HD])
    wo_in = dr("wo_in", [128, D])
    bqkv_in = dr("bqkv_in", [128, 4])
    bo_in = dr("bo_in", [128, DC])
    ones_in = dr("ones_in", [128, 128])
    onescol_in = dr("onescol_in", [128, 2])
    ident_in = dr("ident_in", [128, 128])
    masks_in = dr("masks_in", [128, 4 * T])
    y_out = dr("y_out", [L, D], kind="ExternalOutput")
    xf = dr("xf", [DC, 128, L], kind="Internal")
    h1 = dr("h1", [DC, 128, L], kind="Internal")
    h2 = dr("h2", [DC, 128, L], kind="Internal")
    wbf = dr("wbf", [GPT, 128, 2048], BF16, kind="Internal")
    if debug:
        dbg_h1 = dr("dbg_h1", [DC, 128, L], kind="ExternalOutput")
        dbg_h2 = dr("dbg_h2", [DC, 128, L], kind="ExternalOutput")
        dbg_xf = dr("dbg_xf", [DC, 128, L], kind="ExternalOutput")
        dbg_acc = dr("dbg_acc", [DC, 128, L], kind="ExternalOutput")
        dbg_e = dr("dbg_e", [E, DC, 128, T], kind="ExternalOutput")

    with contextlib.ExitStack() as ctx:
        sb = lambda n, shp, dt: ctx.enter_context(nc.sbuf_tensor(n, shp, dt))
        psa = lambda n, shp, dt=F32: ctx.enter_context(nc.psum_tensor(n, shp, dt))
        sem = lambda n: ctx.enter_context(nc.semaphore(n))

        wstream = sb("wstream", [128, NBUF * 2048], BF16)
        xn = sb("xn", [128, 2 * DC * T], BF16)
        hp = sb("hp", [128, DC * T], BF16)
        acc = sb("acc", [128, DC * T], F32)
        sch = sb("sch", [128, 2 * T], BF16)
        t1ch = sb("t1ch", [128, 2 * T], BF16)
        xch = sb("xch", [128, 4 * T], F32)
        x2ch = sb("x2ch", [128, 2 * T], F32)
        otb = sb("otb", [128, 2 * 2048], F32)
        rows = sb("rows", [1, 6 * T], F32)
        mub = sb("mub", [128, T], F32)
        rb = sb("rb", [128, T], F32)
        gwsb = sb("gwsb", [128, DC * 7], BF16)
        gbc = sb("gbc", [128, 8], F32)
        bbc = sb("bbc", [128, 8], F32)
        gsc = sb("gsc", [128, 16], F32)
        gt = sb("gt", [128, 8 * 7], F32)
        wTsb = sb("wTsb", [7, 4 * 128], F32)
        wbt = sb("wbt", [128, 2 * 8 * T], BF16)
        onescol = sb("onescol", [128, 2], F32)
        onesrow_b = sb("onesrow_b", [1, 128], BF16)
        onesrow_f = sb("onesrow_f", [1, 128], F32)
        ident = sb("ident", [128, 128], F32)
        identb = sb("identb", [128, 128], BF16)
        masks = sb("masks", [128, 4 * T], BF16)
        wqkv = sb("wqkv", [128, 3 * D], BF16)
        wo = sb("wo", [128, D], BF16)
        bqkv = sb("bqkv", [128, 4], F32)
        bo = sb("bo", [128, DC], F32)
        arow = sb("arow", [128, 8], F32)

        qsb = wstream[:, 0:L]
        ksb = wstream[:, L:2 * L]
        vT = wstream[:, 2 * L:3 * L]
        wrow = sb("wrow", [1, 7 * T], BF16)

        ps_rot = psa("ps_rot", [128, 4 * 512])
        ps_tr = psa("ps_tr", [128, 512])
        ps_trb = psa("ps_trb", [128, 1024], BF16)
        ps_wa = psa("ps_wa", [128, 512])
        ps_wb = psa("ps_wb", [128, 512])

        s_row = sem("s_row")
        s_mm, s_sc, s_ve = sem("s_mm"), sem("s_sc"), sem("s_ve")
        s_cst = sem("s_cst")
        s_acst = sem("s_acst")
        s_dbg = sem("s_dbg")
        s_racc = sem("s_racc")
        SC = [sem(f"s_xc{i}") for i in range(4)]
        SW = [sem(f"s_wl{i}") for i in range(NBUF)]
        STE = [sem(f"s_te{i}") for i in range(NBUF)]
        SO = [sem(f"s_ot{i}") for i in range(2)]
        SY = [sem(f"s_yo{i}") for i in range(2)]
        SX2 = [sem(f"s_x2{i}") for i in range(2)]

        p = Prog()
        c_row, c_mm, c_sc, c_ve = (Ctr() for _ in range(4))
        c_cst, c_acst, c_dbg, c_racc = Ctr(), Ctr(), Ctr(), Ctr()
        cSC = [Ctr() for _ in range(4)]
        cSW = [Ctr() for _ in range(NBUF)]
        cSTE = [Ctr() for _ in range(NBUF)]
        cSO = [Ctr() for _ in range(2)]
        cSY = [Ctr() for _ in range(2)]
        cSX2 = [Ctr() for _ in range(2)]
        p.vsem, p.vctr = s_ve, c_ve

        slot_free = [None] * 4
        slot_ctr = [0]

        def take_slot():
            sl = slot_ctr[0] % 4
            slot_ctr[0] += 1
            if slot_free[sl] is not None:
                p.wait("t", *slot_free[sl])
            return sl

        rot = lambda sl: ps_rot[:, sl * 512:(sl + 1) * 512]

        xnv = lambda b: xn[:, b * DC * T:(b + 1) * DC * T]
        wbtv = lambda b: wbt[:, b * 8 * T:(b + 1) * 8 * T]
        GT = {n: gt[:, i * 7:(i + 1) * 7] for i, n in
              enumerate(["log", "pp", "b", "eqm", "b2", "mask", "pg", "w05"])}
        COL = {n: gsc[:, i:i + 1] for i, n in enumerate(["m1", "m2", "m3", "sum", "r"])}
        ROWS = {n: rows[0:1, i * T:(i + 1) * T] for i, n in
                enumerate(["mu", "var", "sig", "rstd", "tmp", "tmp2"])}

        mm_of_group = {}
        wld_tgt, tee_tgt = {}, {}
        t1_of_par = {}
        xch_bar = {"mm": 0, "sc": 0, "ve": 0}

        for f in [
            lambda e: e.dma_start(out=onescol[:], in_=onescol_in),
            lambda e: e.dma_start(out=ident[:], in_=ident_in),
            lambda e: e.dma_start(out=gbc[:], in_=gbb_in),
            lambda e: e.dma_start(out=bbc[:], in_=bbb_in),
            lambda e: e.dma_start(out=bqkv[:], in_=bqkv_in),
            lambda e: e.dma_start(out=bo[:], in_=bo_in),
            lambda e: e.dma_start(out=onesrow_f[:], in_=ones_in[0:1, :]),
        ]:
            p.emit("y", lambda e, f=f: f(e).then_inc(s_cst, 16))
            c_cst.inc(16)
        for f in [
            lambda e: e.dma_start(out=masks[:], in_=masks_in),
            lambda e: e.dma_start(out=gwsb[:], in_=gw_in),
            lambda e: e.dma_start(out=onesrow_b[:], in_=ones_in[0:1, :]),
            lambda e: e.dma_start(out=identb[:], in_=ident_in),
            lambda e: e.dma_start(out=wqkv[:, 0:D], in_=wq_in),
            lambda e: e.dma_start(out=wqkv[:, D:2 * D], in_=wk_in),
            lambda e: e.dma_start(out=wqkv[:, 2 * D:3 * D], in_=wv_in),
            lambda e: e.dma_start(out=wo[:], in_=wo_in),
        ]:
            p.emit("g", lambda e, f=f: f(e).then_inc(s_cst, 16))
            c_cst.inc(16)
        consts_ld = c_cst.v
        p.vstep(lambda e: e.memset(wbt[:, 0:T], 0.5))
        ve_wbt0 = p.vstep(lambda e: e.memset(wbt[:, 8 * T:9 * T], 0.5))

        p.wait("t", s_cst, consts_ld)
        ent_mm, ent_ev, ent_st = {}, {}, {}
        acc_r = acc[:].rearrange("p (d t) -> p d t", d=DC)
        pst_r = ps_tr[:].rearrange("p (a t) -> p a t", a=4)
        for tile in range(NT):
            for tcit in range(4):
                tc = tile * 4 + tcit
                ob = tc % 2

                def lde(e, tc=tc, ob=ob):
                    if tc >= 2:
                        e.wait_ge(s_mm, ent_mm[tc - 2])
                    e.dma_start(out=otb[:, ob * 2048:(ob + 1) * 2048],
                                in_=x_in[tc * 128:(tc + 1) * 128, :]).then_inc(SO[ob], 16)
                p.emit("y", lde)
                cSO[ob].inc(16)
                p.wait("t", SO[ob], cSO[ob].v)
                for dcg in range(4):
                    if (tc, dcg) != (0, 0):
                        prev = ent_ev.get((tc, dcg - 1), ent_ev.get((tc - 1, 3)))
                        p.wait("t", s_ve, prev)
                    for j in range(4):
                        dc = dcg * 4 + j

                        def trf(e, ob=ob, dc=dc, j=j, fin=(j == 3)):
                            ins = e.transpose(ps_tr[:, j * 128:(j + 1) * 128],
                                              otb[:, ob * 2048 + dc * 128: ob * 2048 + (dc + 1) * 128],
                                              ident[:])
                            if fin:
                                ins.then_inc(s_mm, 1)
                        p.emit("t", trf)
                    c_mm.inc()
                    ent_mm[tc] = c_mm.v
                    p.wait("v", s_mm, c_mm.v)
                    if tcit == 0 and dcg == 0 and tile >= 1:
                        p.wait("v", SX2[(tile - 1) % 2], ent_st[tile - 1])
                    ent_ev[(tc, dcg)] = p.vstep(lambda e, dcg=dcg, tcit=tcit: e.tensor_copy(
                        acc_r[:, dcg * 4:(dcg + 1) * 4, tcit * 128:(tcit + 1) * 128],
                        pst_r[:]))
            for dc in range(DC):
                def ste(e, tile=tile, dc=dc, tgt=ent_ev[(tile * 4 + 3, 3)]):
                    if dc == 0:
                        e.wait_ge(s_ve, tgt)
                    e.dma_start(out=xf[dc][:, tile * T:(tile + 1) * T],
                                in_=acc[:, dc * T:(dc + 1) * T]).then_inc(SX2[tile % 2], 16)
                p.emit("y", ste)
                cSX2[tile % 2].inc(16)
            ent_st[tile] = cSX2[tile % 2].v

        def ln_tile(src, tile, xb):
            mm_now, ve_now = c_mm.v, c_ve.v
            x2w = [cSX2[0].v, cSX2[1].v]
            mm_stats, sq_at = {}, {}
            sl_a, sl_b = take_slot(), take_slot()
            for dc in range(DC):
                b4, b2 = dc % 4, dc % 2

                def ldx(e, dc=dc, b4=b4, bar=dict(xch_bar)):
                    if dc == 0:
                        if bar["mm"]:
                            e.wait_ge(s_mm, bar["mm"])
                        if bar["ve"]:
                            e.wait_ge(s_ve, bar["ve"])
                        if bar["sc"]:
                            e.wait_ge(s_sc, bar["sc"])
                    if dc >= 4:
                        e.wait_ge(s_mm, mm_stats[dc - 4])
                        e.wait_ge(s_sc, sq_at[dc - 4])
                    e.dma_start(out=xch[:, b4 * T:(b4 + 1) * T],
                                in_=src[dc][:, tile * T:(tile + 1) * T]).then_inc(SC[b4], 16)
                p.emit("y", ldx)
                cSC[b4].inc(16)
                ld_tgt = cSC[b4].v
                if dc == 0:
                    p.wait("s", s_ve, ve_now)
                    p.wait("s", SX2[0], x2w[0])
                    p.wait("s", SX2[1], x2w[1])
                p.wait("s", SC[b4], ld_tgt)
                if dc >= 2:
                    p.wait("s", s_mm, mm_stats[dc - 2])
                p.emit("s", lambda e, b4=b4, b2=b2: e.activation(
                    x2ch[:, b2 * T:(b2 + 1) * T], xch[:, b4 * T:(b4 + 1) * T],
                    AF.Square).then_inc(s_sc, 1))
                c_sc.inc()
                sq_at[dc] = c_sc.v
                p.wait("t", SC[b4], ld_tgt)
                p.emit("t", lambda e, b4=b4, dc=dc, sl_a=sl_a: e.matmul(
                    rot(sl_a)[0:1, :], onescol[:, 0:1], xch[:, b4 * T:(b4 + 1) * T],
                    start=(dc == 0), stop=(dc == DC - 1)))
                p.wait("t", s_sc, c_sc.v)
                p.emit("t", lambda e, b2=b2, dc=dc, sl_b=sl_b: e.matmul(
                    rot(sl_b)[0:1, :], onescol[:, 0:1], x2ch[:, b2 * T:(b2 + 1) * T],
                    start=(dc == 0), stop=(dc == DC - 1)).then_inc(s_mm, 1))
                c_mm.inc()
                mm_stats[dc] = c_mm.v
            p.wait("v", s_mm, c_mm.v)
            p.vstep(lambda e: e.tensor_scalar(ROWS["mu"], rot(sl_a)[0:1, :],
                                              1.0 / D, None, op0=ALU.mult))
            p.vstep(lambda e: e.tensor_scalar(ROWS["var"], rot(sl_b)[0:1, :],
                                              1.0 / D, 1e-5, op0=ALU.mult, op1=ALU.add))
            p.vstep(lambda e: e.tensor_tensor(ROWS["tmp"], ROWS["mu"], ROWS["mu"], op=ALU.mult))
            tgt = p.vstep(lambda e: e.tensor_tensor(ROWS["var"], ROWS["var"], ROWS["tmp"],
                                                    op=ALU.subtract))
            slot_free[sl_a] = slot_free[sl_b] = (s_ve, tgt)
            p.wait("s", s_ve, tgt)
            p.emit("s", lambda e: e.activation(ROWS["sig"], ROWS["var"], AF.Sqrt).then_inc(s_sc, 1))
            c_sc.inc()
            p.wait("v", s_sc, c_sc.v)
            ve_r = p.vstep(lambda e: e.reciprocal(ROWS["rstd"], ROWS["sig"]))
            sl_c, sl_d = take_slot(), take_slot()
            p.wait("t", s_ve, ve_r)
            p.emit("t", lambda e: e.matmul(rot(sl_c), onesrow_f[:], ROWS["mu"],
                                           start=True, stop=True))
            p.emit("t", lambda e: e.matmul(rot(sl_d), onesrow_f[:], ROWS["rstd"],
                                           start=True, stop=True).then_inc(s_mm, 1))
            c_mm.inc()
            p.wait("v", s_mm, c_mm.v)
            p.vstep(lambda e: e.tensor_copy(mub[:], rot(sl_c)))
            ve_bc = p.vstep(lambda e: e.tensor_copy(rb[:], rot(sl_d)))
            slot_free[sl_c] = slot_free[sl_d] = (s_ve, ve_bc)
            norm_at = {}
            for dc in range(DC):
                b4 = dc % 4

                def ldx2(e, dc=dc, b4=b4):
                    e.wait_ge(s_ve, ve_bc if dc < 4 else norm_at[dc - 4])
                    e.dma_start(out=xch[:, b4 * T:(b4 + 1) * T],
                                in_=src[dc][:, tile * T:(tile + 1) * T]).then_inc(SC[b4], 16)
                p.emit("y", ldx2)
                cSC[b4].inc(16)
                p.wait("v", SC[b4], cSC[b4].v)
                p.vstep(lambda e, b4=b4: e.tensor_tensor(
                    x2ch[:, 0:T], xch[:, b4 * T:(b4 + 1) * T], mub[:], op=ALU.subtract))
                norm_at[dc] = p.vstep(lambda e, dc=dc, xb=xb: e.tensor_tensor(
                    xnv(xb)[:, dc * T:(dc + 1) * T], x2ch[:, 0:T], rb[:], op=ALU.mult))
            xch_bar["mm"] = mm_stats[DC - 1]
            xch_bar["sc"] = sq_at[DC - 1]
            xch_bar["ve"] = norm_at[DC - 1]
            return norm_at[DC - 1]

        def gating_tile(ve_xn, xb):
            mm_now, ve_now = c_mm.v, c_ve.v
            gate_done, logtt, wb_at = {}, {}, {}
            for tc in range(4):
                sl = take_slot()
                p.wait("t", s_ve, ve_xn if tc == 0 else logtt[tc - 1])
                for dc in range(DC):
                    p.emit("t", lambda e, dc=dc, tc=tc, sl=sl, xb=xb: e.matmul(
                        rot(sl)[:, 0:7], xnv(xb)[:, dc * T + tc * 128: dc * T + (tc + 1) * 128],
                        gwsb[:, dc * 7:(dc + 1) * 7], start=(dc == 0), stop=(dc == DC - 1)))
                p.emit("t", lambda e, sl=sl, xb=xb: e.matmul(
                    rot(sl)[:, 8:512], xnv(xb)[:, 0:128], xnv(xb)[:, 0:504],
                    start=False, stop=True).then_inc(s_mm, 1))
                c_mm.inc()
                p.wait("v", s_mm, c_mm.v)
                p.vstep(lambda e, sl=sl: e.tensor_tensor(GT["log"], rot(sl)[:, 0:7],
                                                         gbc[:, 0:7], op=ALU.add))
                logtt[tc] = c_ve.v
                slot_free[sl] = (s_ve, logtt[tc])
                p.vstep(lambda e: e.reduce_max(COL["m1"], GT["log"], axis=AXX))
                p.vstep(lambda e: e.tensor_scalar(GT["pp"], GT["log"], COL["m1"], None,
                                                  op0=ALU.subtract))
                p.wait("s", s_ve, c_ve.v)
                p.emit("s", lambda e: e.activation(GT["pp"], GT["pp"], AF.Exp,
                                                   accum_out=COL["sum"]).then_inc(s_sc, 1))
                c_sc.inc()
                p.wait("v", s_sc, c_sc.v)
                for f in [
                    lambda e: e.reciprocal(COL["r"], COL["sum"]),
                    lambda e: e.tensor_scalar(GT["pp"], GT["pp"], COL["r"], None, op0=ALU.mult),
                    lambda e: e.tensor_tensor(GT["b"], GT["pp"], bbc[:, 0:7], op=ALU.add),
                    lambda e: e.reduce_max(COL["m1"], GT["b"], axis=AXX),
                    lambda e: e.tensor_scalar(GT["eqm"], GT["b"], COL["m1"], NEG,
                                              op0=ALU.is_ge, op1=ALU.mult),
                    lambda e: e.tensor_tensor(GT["b2"], GT["b"], GT["eqm"], op=ALU.add),
                    lambda e: e.reduce_max(COL["m2"], GT["b2"], axis=AXX),
                    lambda e: e.tensor_scalar(GT["eqm"], GT["b2"], COL["m2"], NEG,
                                              op0=ALU.is_ge, op1=ALU.mult),
                    lambda e: e.tensor_tensor(GT["b2"], GT["b2"], GT["eqm"], op=ALU.add),
                    lambda e: e.reduce_max(COL["m3"], GT["b2"], axis=AXX),
                    lambda e: e.tensor_scalar(GT["mask"], GT["b"], COL["m3"], None, op0=ALU.is_ge),
                    lambda e: e.tensor_tensor(GT["pg"], GT["pp"], GT["mask"], op=ALU.mult),
                    lambda e: e.reduce_sum(COL["sum"], GT["pg"], axis=AXX),
                    lambda e: e.reciprocal(COL["r"], COL["sum"]),
                ]:
                    p.vstep(f)
                ve_w = p.vstep(lambda e: e.tensor_scalar(GT["w05"], GT["pg"], COL["r"], 0.5,
                                                         op0=ALU.mult, op1=ALU.mult))
                p.wait("t", s_ve, ve_w)
                p.emit("t", lambda e: e.transpose(ps_tr[0:7, 0:128], GT["w05"],
                                                  ident[:]).then_inc(s_mm, 1))
                c_mm.inc()
                p.wait("v", s_mm, c_mm.v)
                gate_done[tc] = p.vstep(lambda e, tc=tc: e.tensor_copy(
                    wTsb[:, tc * 128:(tc + 1) * 128], ps_tr[0:7, 0:128]))
            ve_gate = gate_done[3]
            for ex in range(7):
                def rowdma(e, ex=ex):
                    if ex == 0:
                        e.wait_ge(s_ve, ve_gate)
                        e.wait_ge(s_mm, mm_now)
                    e.dma_start(out=wrow[0:1, ex * T:(ex + 1) * T],
                                in_=wTsb[ex:ex + 1, :]).then_inc(s_row, 16)
                p.emit("g", rowdma)
                c_row.inc(16)
            p.wait("t", s_row, c_row.v)
            p.wait("t", s_ve, ve_gate)
            for ex in range(7):
                sl = take_slot()
                p.emit("t", lambda e, ex=ex, sl=sl: e.matmul(
                    rot(sl), onesrow_b[:], wrow[0:1, ex * T:(ex + 1) * T],
                    start=True, stop=True).then_inc(s_mm, 1))
                c_mm.inc()
                p.wait("s", s_mm, c_mm.v)
                if ex == 0:
                    p.wait("s", s_ve, ve_now)
                p.emit("s", lambda e, ex=ex, sl=sl, xb=xb: e.copy(
                    wbtv(xb)[:, (ex + 1) * T:(ex + 2) * T], rot(sl)).then_inc(s_sc, 1))
                c_sc.inc()
                wb_at[ex] = c_sc.v
                slot_free[sl] = (s_sc, c_sc.v)
            return wb_at

        def ffn_tile(call, tile, ve_xn, wb_at, prev_acc_reader, xb, hooks=None):
            yev = {}
            first_pass = (call == 0 and tile == 0)
            g0 = call * NT * GPT + tile * GPT

            mm_before = c_mm.v if (call == 1 and tile == 0) else None

            def emit_prefetch_range(k0, k1):
              for k in range(k0, k1):
                gg = g0 + k

                b = gg % NBUF
                tee_barrier = cSTE[:] if (mm_before is not None and k == 0) else None
                tee_barrier_v = [c.v for c in cSTE] if tee_barrier else None

                def pf(e, gg=gg, k=k, b=b, tee_barrier_v=tee_barrier_v):
                    if mm_before is not None and k == 0:
                        e.wait_ge(s_mm, mm_before)
                        for bi in range(NBUF):
                            if tee_barrier_v[bi] > 0:
                                e.wait_ge(STE[bi], tee_barrier_v[bi])
                    if gg >= DEPTH:
                        e.wait_ge(s_mm, mm_of_group[gg - DEPTH])
                    src_blk = wsw[gg % GPT] if first_pass else wbf[gg % GPT]
                    kw = dict(max_dma_last_dim=4096) if first_pass else {}
                    if first_pass and gg >= NBUF:
                        e.wait_ge(STE[b], tee_tgt[gg - NBUF])
                    if (not first_pass) and call == 0:
                        e.wait_ge(STE[b], tee_tgt[gg % GPT])
                    e.dma_start(out=wstream[:, b * 2048:(b + 1) * 2048],
                                in_=src_blk, **kw).then_inc(SW[b], 16)
                p.emit("g", pf)
                cSW[b].inc(16)
                wld_tgt[gg] = cSW[b].v
                if first_pass:
                    def tee(e, gg=gg, b=b, wt=wld_tgt[gg]):
                        e.wait_ge(SW[b], wt)
                        e.dma_start(out=wbf[gg],
                                    in_=wstream[:, b * 2048:(b + 1) * 2048]
                                    ).then_inc(STE[b], 16)
                    p.emit("y", tee)
                    cSTE[b].inc(16)
                    tee_tgt[gg] = cSTE[b].v

            def mm_group(gg, rhs_fn, extra=()):
                sl = take_slot()
                p.wait("t", SW[gg % NBUF], wld_tgt[gg])
                for w in extra:
                    p.wait("t", *w)
                buf = gg % NBUF
                for k in range(DC):
                    def mmf(e, sl=sl, buf=buf, k=k, rhs_fn=rhs_fn):
                        ins = e.matmul(rot(sl),
                                       wstream[:, buf * 2048 + k * 128: buf * 2048 + (k + 1) * 128],
                                       rhs_fn(k), start=(k == 0), stop=(k == DC - 1))
                        if k == DC - 1:
                            ins.then_inc(s_mm, 1)
                    p.emit("t", mmf)
                c_mm.inc()
                mm_of_group[gg] = c_mm.v
                return sl

            gg = g0
            for e_ in range(E):
                if hooks and e_ in hooks:
                    hooks[e_]()
                emit_prefetch_range(e_ * 48, (e_ + 1) * 48)
                for hc in range(DC):
                    sl = mm_group(gg, lambda k, xb=xb: xnv(xb)[:, k * T:(k + 1) * T])
                    p.wait("s", s_mm, c_mm.v)
                    if t1_of_par.get(hc % 2):
                        p.wait("s", s_ve, t1_of_par[hc % 2])
                    p.emit("s", lambda e, sl=sl, hc=hc: e.activation(
                        sch[:, (hc % 2) * T:(hc % 2 + 1) * T], rot(sl),
                        AF.Silu).then_inc(s_sc, 1))
                    c_sc.inc()
                    slot_free[sl] = (s_sc, c_sc.v)
                    sc_silu = c_sc.v
                    gg += 1
                    sl = mm_group(gg, lambda k, xb=xb: xnv(xb)[:, k * T:(k + 1) * T])
                    p.wait("v", s_mm, c_mm.v)
                    p.wait("v", s_sc, sc_silu)
                    tgt_t1 = p.vstep(lambda e, sl=sl, hc=hc: e.tensor_tensor(
                        t1ch[:, (hc % 2) * T:(hc % 2 + 1) * T], rot(sl),
                        sch[:, (hc % 2) * T:(hc % 2 + 1) * T], op=ALU.mult))
                    slot_free[sl] = (s_ve, tgt_t1)
                    t1_of_par[hc % 2] = tgt_t1
                    if e_ > 0 and hc == 0:
                        p.wait("v", s_sc, wb_at[e_ - 1])
                    if first_pass and e_ == 0 and hc == 0:
                        p.wait("v", s_ve, ve_wbt0)
                    p.vstep(lambda e, hc=hc, e_=e_, xb=xb: e.tensor_tensor(
                        hp[:, hc * T:(hc + 1) * T], t1ch[:, (hc % 2) * T:(hc % 2 + 1) * T],
                        wbtv(xb)[:, e_ * T:(e_ + 1) * T], op=ALU.mult))
                    gg += 1
                hp_ready = c_ve.v
                for dc in range(DC):
                    extra = [(s_ve, hp_ready)] if dc == 0 else []
                    if e_ == 0 and dc == 0 and prev_acc_reader is not None:
                        extra.append(prev_acc_reader)
                    sl = mm_group(gg, lambda k: hp[:, k * T:(k + 1) * T], extra)
                    p.wait("v", s_mm, c_mm.v)
                    if e_ == 0:
                        tgt = p.vstep(lambda e, sl=sl, dc=dc: e.tensor_copy(
                            acc[:, dc * T:(dc + 1) * T], rot(sl)))
                    else:
                        tgt = p.vstep(lambda e, sl=sl, dc=dc: e.tensor_tensor(
                            acc[:, dc * T:(dc + 1) * T], rot(sl), acc[:, dc * T:(dc + 1) * T],
                            op=ALU.add))
                    slot_free[sl] = (s_ve, tgt)
                    if e_ == E - 1:
                        yev[dc] = tgt
                    gg += 1
                if debug and call == 1 and tile == 1:
                    for dc in range(DC):
                        def dste(e, e_=e_, dc=dc, tgt=tgt):
                            if dc == 0:
                                e.wait_ge(s_ve, tgt)
                            e.dma_start(out=dbg_e[e_][dc],
                                        in_=acc[:, dc * T:(dc + 1) * T]).then_inc(s_dbg, 16)
                        p.emit("y", dste)
                        c_dbg.inc(16)
            return yev

        def residual_add_tile(tile, src, yev):
            bar = dict(xch_bar)
            add_at = {}
            for dc in range(DC):
                b4 = dc % 4

                def ldr(e, dc=dc, b4=b4, tile=tile, bar=bar):
                    if dc == 0:
                        if bar["mm"]:
                            e.wait_ge(s_mm, bar["mm"])
                        if bar["ve"]:
                            e.wait_ge(s_ve, bar["ve"])
                        if bar["sc"]:
                            e.wait_ge(s_sc, bar["sc"])
                    if dc >= 4:
                        e.wait_ge(s_ve, add_at[dc - 4])
                    e.dma_start(out=xch[:, b4 * T:(b4 + 1) * T],
                                in_=src[dc][:, tile * T:(tile + 1) * T]).then_inc(SC[b4], 16)
                p.emit("y", ldr)
                cSC[b4].inc(16)
                p.wait("v", SC[b4], cSC[b4].v)
                p.wait("v", s_ve, yev[dc])
                add_at[dc] = p.vstep(lambda e, dc=dc, b4=b4: e.tensor_tensor(
                    acc[:, dc * T:(dc + 1) * T], acc[:, dc * T:(dc + 1) * T],
                    xch[:, b4 * T:(b4 + 1) * T], op=ALU.add))
            xch_bar["ve"] = add_at[DC - 1]
            return add_at

        def residual_store_tile(tile, src, dst, yev):
            add_at = residual_add_tile(tile, src, yev)
            for dc in range(DC):
                def stx(e, dc=dc, tile=tile, tgt=add_at[dc]):
                    e.wait_ge(s_ve, tgt)
                    e.dma_start(out=dst[dc][:, tile * T:(tile + 1) * T],
                                in_=acc[:, dc * T:(dc + 1) * T]).then_inc(s_acst, 16)
                p.emit("y", stx)
                c_acst.inc(16)
            return (s_acst, c_acst.v)

        exit_state = {"prev_ev": None, "st": {}}

        def residual_exit_tile(tile, src, yev):
            add_at = residual_add_tile(tile, src, yev)
            p.wait("t", s_ve, add_at[DC - 1])
            if debug:
                for dc in range(DC):
                    def dstx(e, dc=dc, tile=tile, tgt=res_at[dc]):
                        e.wait_ge(s_ve, tgt)
                        e.dma_start(out=dbg_acc[dc][:, tile * T:(tile + 1) * T],
                                    in_=acc[:, dc * T:(dc + 1) * T]).then_inc(s_st, 16)
                    p.emit("y", dstx)
                    c_st.inc(16)
            for tc in range(4):
                ob = tc % 2
                ev_at = {}
                for dcg in range(4):
                    prev = ev_at.get(dcg - 1, exit_state["prev_ev"])
                    if prev is not None:
                        p.wait("t", s_ve, prev)
                    for j in range(4):
                        dc = dcg * 4 + j

                        def trf(e, dc=dc, j=j, tc=tc, fin=(j == 3)):
                            ins = e.transpose(
                                ps_tr[:, j * 128:(j + 1) * 128],
                                acc[:, dc * T + tc * 128: dc * T + (tc + 1) * 128], ident[:])
                            if fin:
                                ins.then_inc(s_mm, 1)
                        p.emit("t", trf)
                    c_mm.inc()
                    p.wait("v", s_mm, c_mm.v)
                    key = (tile, tc - 2) if tc >= 2 else (tile - 1, tc + 2)
                    if dcg == 0 and key in exit_state["st"]:
                        p.wait("v", SY[ob], exit_state["st"][key])
                    ev_at[dcg] = p.vstep(lambda e, ob=ob, dcg=dcg: e.tensor_copy(
                        otb[:, ob * 2048 + dcg * 512: ob * 2048 + (dcg + 1) * 512], ps_tr[:]))
                exit_state["prev_ev"] = ev_at[3]

                def sty(e, tile=tile, tc=tc, ob=ob, tgt=ev_at[3]):
                    e.wait_ge(s_ve, tgt)
                    e.dma_start(out=y_out[tile * T + tc * 128: tile * T + (tc + 1) * 128, :],
                                in_=otb[:, ob * 2048:(ob + 1) * 2048]).then_inc(SY[ob], 16)
                p.emit("y", sty)
                cSY[ob].inc(16)
                exit_state["st"][(tile, tc)] = cSY[ob].v
            return (s_mm, c_mm.v)

        def attention(src, dst, tail_hook=None, pre_ve0=None):
            qkv_done, pv_ev, wo_st = {}, {}, {}
            ve_xn_of = {}
            if pre_ve0 is not None:
                ve_xn_of[0] = pre_ve0

            def attn_ln(sl_):
                p.mark(f"attn.ln{sl_}")
                ve_xn_of[sl_] = ln_tile(src, sl_, sl_ % 2)

            def attn_qkv(sl_):
                p.mark(f"attn.qkv{sl_}")
                for m in range(3):
                    sl = take_slot()
                    if m == 0:
                        p.wait("t", s_ve, ve_xn_of[sl_])
                    for dc in range(DC):
                        def mmq(e, sl=sl, m=m, dc=dc, sl_=sl_):
                            ins = e.matmul(rot(sl),
                                           wqkv[:, m * D + dc * 128: m * D + (dc + 1) * 128],
                                           xnv(sl_ % 2)[:, dc * T:(dc + 1) * T],
                                           start=(dc == 0), stop=(dc == DC - 1))
                            if dc == DC - 1:
                                ins.then_inc(s_mm, 1)
                        p.emit("t", mmq)
                    c_mm.inc()
                    p.wait("v", s_mm, c_mm.v)
                    if m == 0:
                        tgt = p.vstep(lambda e, sl=sl, sl_=sl_: e.tensor_scalar(
                            qsb[:, sl_ * T:(sl_ + 1) * T], rot(sl), bqkv[:, 0:1], ISQ,
                            op0=ALU.add, op1=ALU.mult))
                    elif m == 1:
                        tgt = p.vstep(lambda e, sl=sl, sl_=sl_: e.tensor_scalar(
                            ksb[:, sl_ * T:(sl_ + 1) * T], rot(sl), bqkv[:, 1:2], None, op0=ALU.add))
                    else:
                        tgt = p.vstep(lambda e, sl=sl, sl_=sl_: e.tensor_scalar(
                            sch[:, 0:T], rot(sl), bqkv[:, 2:3], None, op0=ALU.add))
                    slot_free[sl] = (s_ve, tgt)
                p.wait("t", s_ve, c_ve.v)
                for j in range(4):
                    def trv(e, j=j, fin=(j == 3)):
                        ins = e.transpose(ps_trb[:, j * 128:(j + 1) * 128],
                                          sch[:, j * 128:(j + 1) * 128], identb[:])
                        if fin:
                            ins.then_inc(s_mm, 1)
                    p.emit("t", trv)
                c_mm.inc()
                p.wait("v", s_mm, c_mm.v)
                qkv_done[sl_] = p.vstep(lambda e, sl_=sl_: e.tensor_copy(
                    vT[:, sl_ * T:(sl_ + 1) * T], ps_trb[:, 0:512]))

            trB_done = {}

            def attn_rows_A(s_, qi):
                half = qi % 2
                ab = half * 3
                sneg = acc[:, half * 4096: half * 4096 + 4096]
                nks = s_ + 1
                qc = s_ * 4 + qi
                last_tgt = None
                first_ts = True
                for ks in range(nks):
                    sl = take_slot()
                    p.emit("t", lambda e, sl=sl, qc=qc, ks=ks: e.matmul(
                        rot(sl), qsb[:, qc * 128:(qc + 1) * 128],
                        ksb[:, ks * T:(ks + 1) * T], start=True,
                        stop=True).then_inc(s_mm, 1))
                    c_mm.inc()
                    p.wait("v", s_mm, c_mm.v)
                    if first_ts and trB_done.get(half) is not None:
                        p.wait("v", s_mm, trB_done[half])
                    first_ts = False
                    first = ks == 0
                    if ks < nks - 1:
                        last_tgt = p.vstep(lambda e, sl=sl, ks=ks, first=first, sneg=sneg, ab=ab:
                                           e.tensor_scalar(
                            sneg[:, ks * T:(ks + 1) * T], rot(sl), -1.0,
                            None if first else arow[:, ab:ab + 1],
                            op0=ALU.mult, op1=ALU.min, accum_out=arow[:, ab:ab + 1]))
                    else:
                        p.vstep(lambda e, sl=sl, qi=qi: e.tensor_tensor(
                            x2ch[:, 0:T], rot(sl), masks[:, qi * T:(qi + 1) * T], op=ALU.add))
                        last_tgt = p.vstep(lambda e, ks=ks, first=first, sneg=sneg, ab=ab:
                                           e.tensor_scalar(
                            sneg[:, ks * T:(ks + 1) * T], x2ch[:, 0:T], -1.0,
                            None if first else arow[:, ab:ab + 1],
                            op0=ALU.mult, op1=ALU.min, accum_out=arow[:, ab:ab + 1]))
                    slot_free[sl] = (s_ve, last_tgt)
                p.wait("s", s_ve, last_tgt)
                p.emit("s", lambda e, nks=nks, sneg=sneg, ab=ab: e.activation(
                    sneg[:, 0:nks * T], sneg[:, 0:nks * T], AF.Exp, scale=-1.0,
                    bias=arow[:, ab:ab + 1], accum_out=arow[:, ab + 1:ab + 2]).then_inc(s_sc, 1))
                c_sc.inc()
                p.wait("v", s_sc, c_sc.v)
                p.vstep(lambda e, ab=ab: e.reciprocal(arow[:, ab + 2:ab + 3], arow[:, ab + 1:ab + 2]))
                ve_p = p.vstep(lambda e, nks=nks, sneg=sneg, ab=ab: e.tensor_scalar(
                    sneg[:, 0:nks * T], sneg[:, 0:nks * T], arow[:, ab + 2:ab + 3],
                    None, op0=ALU.mult))
                return ve_p

            def attn_rows_B(s_, qi, ve_p):
                half = qi % 2
                sneg = acc[:, half * 4096: half * 4096 + 4096]
                nks = s_ + 1
                nkc = nks * 4
                p.wait("t", s_ve, ve_p)
                ptr_ev = {}
                for kc in range(nkc):
                    jj = kc % 4
                    if jj == 0 and kc >= 4:
                        p.wait("t", s_ve, ptr_ev[kc - 4])
                    fin = (jj == 3 or kc == nkc - 1)

                    def trp(e, kc=kc, jj=jj, fin=fin, sneg=sneg):
                        ins = e.transpose(ps_tr[:, jj * 128:(jj + 1) * 128],
                                          sneg[:, kc * 128:(kc + 1) * 128], ident[:])
                        if fin:
                            ins.then_inc(s_mm, 1)
                    p.emit("t", trp)
                    if fin:
                        c_mm.inc()
                        p.wait("v", s_mm, c_mm.v)
                        lo = (kc // 4) * 4
                        width = (kc - lo + 1) * 128
                        ptr_ev[lo] = p.vstep(lambda e, lo=lo, width=width: e.tensor_copy(
                            hp[:, lo * 128: lo * 128 + width], ps_tr[:, 0:width]))
                trB_done[half] = c_mm.v
                ve_ptb = ptr_ev[(nkc - 1) // 4 * 4]
                pv_ps = ps_wa if qi % 2 == 0 else ps_wb
                p.wait("t", s_ve, ve_ptb)
                if pv_ev.get(qi % 2):
                    p.wait("t", s_ve, pv_ev[qi % 2])
                p.wait("t", s_ve, qkv_done[s_])
                for kc in range(nkc):
                    def mpv(e, kc=kc, pv_ps=pv_ps, nkc=nkc):
                        ins = e.matmul(pv_ps[:, 0:128], vT[:, kc * 128:(kc + 1) * 128],
                                       hp[:, kc * 128:(kc + 1) * 128],
                                       start=(kc == 0), stop=(kc == nkc - 1))
                        if kc == nkc - 1:
                            ins.then_inc(s_mm, 1)
                    p.emit("t", mpv)
                c_mm.inc()
                trB_done[half] = c_mm.v
                p.wait("v", s_mm, c_mm.v)
                pv_ev[qi % 2] = p.vstep(lambda e, pv_ps=pv_ps, qi=qi: e.tensor_copy(
                    sch[:, T + qi * 128: T + (qi + 1) * 128], pv_ps[:, 0:128]))

            def attn_rows_wo(s_):
                p.mark(f"attn.rows{s_}")
                p.wait("v", SX2[0], cSX2[0].v)
                p.wait("v", SX2[1], cSX2[1].v)
                vp = {}
                vp[0] = attn_rows_A(s_, 0)
                vp[1] = attn_rows_A(s_, 1)
                attn_rows_B(s_, 0, vp[0])
                vp[2] = attn_rows_A(s_, 2)
                attn_rows_B(s_, 1, vp[1])
                vp[3] = attn_rows_A(s_, 3)
                attn_rows_B(s_, 2, vp[2])
                attn_rows_B(s_, 3, vp[3])
                attnF = sch[:, T:2 * T]
                p.mark(f"attn.wo{s_}")
                for dc in range(DC):
                    sl = take_slot()
                    if dc == 0:
                        p.wait("t", s_ve, pv_ev[1])
                    p.emit("t", lambda e, sl=sl, dc=dc: e.matmul(
                        rot(sl), wo[:, dc * 128:(dc + 1) * 128], attnF,
                        start=True, stop=True).then_inc(s_mm, 1))
                    c_mm.inc()
                    p.wait("v", s_mm, c_mm.v)
                    if dc >= 2:
                        p.wait("v", SX2[dc % 2], wo_st[dc - 2])
                    elif wo_st.get(dc + 14):
                        p.wait("v", SX2[dc % 2], wo_st[dc + 14])
                    tgt = p.vstep(lambda e, sl=sl, dc=dc: e.tensor_scalar(
                        x2ch[:, (dc % 2) * T:(dc % 2 + 1) * T], rot(sl), bo[:, dc:dc + 1],
                        None, op0=ALU.add))
                    slot_free[sl] = (s_ve, tgt)

                    def rac(e, dc=dc, s=s_, tgt=tgt):
                        e.wait_ge(s_ve, tgt)
                        e.dma_start(out=x2ch[:, (dc % 2) * T:(dc % 2 + 1) * T],
                                    in_=src[dc][:, s * T:(s + 1) * T],
                                    accum_op=ALU.add).then_inc(s_racc, 16)
                    p.emit("g", rac)
                    c_racc.inc(16)

                    def sth(e, dc=dc, s=s_, tgt=c_racc.v):
                        e.wait_ge(s_racc, tgt)
                        e.dma_start(out=dst[dc][:, s * T:(s + 1) * T],
                                    in_=x2ch[:, (dc % 2) * T:(dc % 2 + 1) * T]).then_inc(SX2[dc % 2], 16)
                    p.emit("y", sth)
                    cSX2[dc % 2].inc(16)
                    wo_st[dc] = cSX2[dc % 2].v

            if 0 not in ve_xn_of:
                attn_ln(0)
            attn_qkv(0)
            for s_ in range(NT):
                if s_ + 1 < NT:
                    attn_ln(s_ + 1)
                if s_ == NT - 1 and tail_hook is not None:
                    tail_hook()
                attn_rows_wo(s_)
                if s_ + 1 < NT:
                    attn_qkv(s_ + 1)

        XF = [xf[dc] for dc in range(DC)]
        H1 = [h1[dc] for dc in range(DC)]
        H2 = [h2[dc] for dc in range(DC)]

        def run_ffn_call(call, src, dst, prev, pre=None, hooks_last=None):
            nxt = {}
            if pre is None:
                p.mark(f"c{call}.ln0")
                ve_xn = ln_tile(src, 0, 0)
                wb_at = gating_tile(ve_xn, 0)
            else:
                ve_xn, wb_at = pre
            for tile in range(NT):
                xb = tile % 2
                first_pass = (call == 0 and tile == 0)
                hooks = None
                if tile + 1 < NT and not first_pass:
                    nb = (tile + 1) % 2
                    def mk_ln(tile=tile, nb=nb):
                        p.mark(f"c{call}.pln{tile + 1}")
                        nxt["ve"] = ln_tile(src, tile + 1, nb)
                    def mk_gate(nb=nb):
                        nxt["wb"] = gating_tile(nxt["ve"], nb)
                    hooks = {2: mk_ln, 4: mk_gate}
                elif tile == NT - 1 and hooks_last is not None:
                    hooks = hooks_last
                p.mark(f"c{call}.exp{tile}")
                yev = ffn_tile(call, tile, ve_xn, wb_at, prev, xb, hooks)
                p.mark(f"c{call}.res{tile}")
                if dst is not None:
                    prev = residual_store_tile(tile, src, dst, yev)
                else:
                    prev = residual_exit_tile(tile, src, yev)
                if tile + 1 < NT:
                    if first_pass:
                        nxt["ve"] = ln_tile(src, tile + 1, (tile + 1) % 2)
                        nxt["wb"] = gating_tile(nxt["ve"], (tile + 1) % 2)
                    ve_xn, wb_at = nxt["ve"], nxt["wb"]
            return prev

        attn_pre = {}

        def mk_attn_ln0():
            p.mark("attn.pln0")
            attn_pre["ve0"] = ln_tile(H1, 0, 0)
        run_ffn_call(0, XF, H1, None, hooks_last={2: mk_attn_ln0})
        p.mark("attn")
        f2pre = {}

        def tail():
            p.mark("c1.ln0")
            ve0 = ln_tile(H2, 0, 0)
            f2pre["pre"] = (ve0, gating_tile(ve0, 0))
        attention(H1, H2, tail_hook=tail, pre_ve0=attn_pre["ve0"])
        run_ffn_call(1, H2, None, (s_sc, c_sc.v), pre=f2pre["pre"])

        if debug:
            def dmp(e):
                e.dma_start(out=dbg_xf, in_=xf).then_inc(s_dbg, 16)
                e.dma_start(out=dbg_h1, in_=h1).then_inc(s_dbg, 16)
                e.dma_start(out=dbg_h2, in_=h2).then_inc(s_dbg, 16)
            p.emit("y", dmp)
            c_dbg.inc(48)
            p.wait("y", s_dbg, c_dbg.v)
        p.wait("y", SY[0], cSY[0].v)
        p.wait("y", SY[1], cSY[1].v)
        p.wait("y", s_acst, c_acst.v)

        semnames = {id(s_): n for s_, n in
                    [(s_row, "row"), (s_mm, "mm"), (s_sc, "sc"), (s_ve, "ve"),
                     (s_cst, "cst"), (s_acst, "acst"), (s_dbg, "dbg"), (s_racc, "racc")] +
                    [(SC[i], f"xc{i}") for i in range(4)] +
                    [(SW[i], f"wl{i}") for i in range(NBUF)] +
                    [(STE[i], f"te{i}") for i in range(NBUF)] +
                    [(SO[i], f"ot{i}") for i in range(2)] +
                    [(SY[i], f"yo{i}") for i in range(2)] +
                    [(SX2[i], f"x2{i}") for i in range(2)]}
        simulate(p, semnames)

        with nc.Block() as block:
            @block.tensor
            def _(t):
                for f in p.q["t"]:
                    f(t)

            @block.vector
            def _(v):
                for f in p.q["v"]:
                    f(v)

            @block.scalar
            def _(s):
                for f in p.q["s"]:
                    f(s)

            @block.sync
            def _(y):
                for f in p.q["y"]:
                    f(y)

            @block.gpsimd
            def _(g):
                for f in p.q["g"]:
                    f(g)
    return nc


def _swz(w):
    return np.ascontiguousarray(
        np.asarray(w, np.float32).reshape(16, 128, 16, 128).transpose(2, 1, 0, 3)
    ).reshape(16, 128, 2048)


def host_stage(inputs):
    f = lambda k: np.asarray(inputs[k], np.float32)
    W1, Wg, W2 = f("L0_ffn_W1"), f("L0_ffn_Wg"), f("L0_ffn_W2")
    blocks = []
    for e_ in range(E):
        if e_ == 0:
            a, b, c = f("L0_ffn_Ws1"), f("L0_ffn_Wsg"), f("L0_ffn_Ws2")
        else:
            a, b, c = W1[e_ - 1], Wg[e_ - 1], W2[e_ - 1]
        asw, bsw, csw = _swz(a), _swz(b), _swz(c)
        for hc in range(16):
            blocks.append(asw[hc])
            blocks.append(bsw[hc])
        for dc in range(16):
            blocks.append(csw[dc])
    wsw = np.ascontiguousarray(np.stack(blocks))

    gate_w = f("L0_ffn_gate_w")
    gw = np.ascontiguousarray(gate_w.reshape(16, 128, 7).transpose(1, 0, 2).reshape(128, 112))
    pad8 = lambda v: np.tile(
        np.concatenate([np.asarray(v, np.float32), [0.0]]).astype(np.float32), (128, 1))
    masks = np.zeros((4, 128, T), np.float32)
    q_idx = np.arange(128)[:, None]
    k_idx = np.arange(T)[None, :]
    for r in range(4):
        masks[r] = np.where(k_idx <= r * 128 + q_idx, 0.0, NEG).astype(np.float32)
    masks = np.ascontiguousarray(masks.transpose(1, 0, 2).reshape(128, 4 * T))

    common = dict(
        wsw=wsw,
        gw_in=gw,
        gbb_in=pad8(f("L0_ffn_gate_b")),
        bbb_in=pad8(f("L0_ffn_biases")),
        wq_in=np.ascontiguousarray(f("L0_attn_Wq").reshape(DC, 128, HD).transpose(1, 0, 2).reshape(128, DC * HD)),
        wk_in=np.ascontiguousarray(f("L0_attn_Wk").reshape(DC, 128, HD).transpose(1, 0, 2).reshape(128, DC * HD)),
        wv_in=np.ascontiguousarray(f("L0_attn_Wv").reshape(DC, 128, HD).transpose(1, 0, 2).reshape(128, DC * HD)),
        wo_in=np.ascontiguousarray(f("L0_attn_Wo")),
        bqkv_in=np.ascontiguousarray(
            np.stack([f("L0_attn_bq"), f("L0_attn_bk"), f("L0_attn_bv"),
                      np.zeros(128, np.float32)], 1)),
        bo_in=np.ascontiguousarray(f("L0_attn_bo").reshape(DC, 128).T),
        ones_in=np.ones((128, 128), np.float32),
        onescol_in=np.ones((128, 2), np.float32),
        ident_in=np.eye(128, dtype=np.float32),
        masks_in=masks,
    )
    x = np.asarray(inputs["L0_attn_x"], np.float32)
    in_maps = []
    for c in range(8):
        m = dict(common)
        m["x_in"] = np.ascontiguousarray(x[c])
        in_maps.append(m)
    return in_maps


def kernel(**inputs):
    in_maps = host_stage(inputs)
    nc = build_nc()
    res = run_bass_kernel_spmd(nc, in_maps, core_ids=list(range(8)))
    return np.stack([res.results[c]["y_out"] for c in range(8)]).astype(np.float32)


if __name__ == "__main__":
    nc = build_nc()
    print("build ok, n_inst =", sum(1 for _ in nc.all_instructions()))


# version 08 (final)
# speedup vs baseline: 1.1295x; 1.1215x over previous
import contextlib
import numpy as np
import concourse.bass as bass
import concourse.mybir as mybir
from concourse.bass_utils import run_bass_kernel_spmd

F32 = mybir.dt.float32
BF16 = mybir.dt.bfloat16
AF = mybir.ActivationFunctionType
ALU = mybir.AluOpType
AXX = mybir.AxisListType.X

D, DC, L, T, NT, E, HD = 2048, 16, 4096, 512, 8, 8, 128
NBUF, DEPTH = 6, 5
GPT = E * 48
NEG = -1.0e30
ISQ = float(1.0 / np.sqrt(128.0))


class Ctr:
    def __init__(self):
        self.v = 0

    def inc(self, n=1):
        self.v += n
        return self.v


class Prog:
    def __init__(self):
        self.q = {k: [] for k in ("t", "v", "s", "y", "g")}
        self.vsem = None
        self.vctr = None
        self.marks = []
        self.cur = "init"
        self.tags = {k: [] for k in ("t", "v", "s", "y", "g")}

    def emit(self, eng, f):
        self.q[eng].append(f)
        self.tags[eng].append(self.cur)

    def wait(self, eng, sem, val):
        if val > 0:
            self.q[eng].append(lambda e, sem=sem, val=val: e.wait_ge(sem, val))
            self.tags[eng].append(self.cur + "|W")

    def mark(self, label):
        self.cur = label
        self.marks.append((label, {k: len(v) for k, v in self.q.items()}))

    def vstep(self, f):
        sem = self.vsem
        self.q["v"].append(lambda e, f=f, sem=sem: f(e).then_inc(sem, 1))
        self.tags["v"].append(self.cur + "|VS")
        tgt = self.vctr.inc()
        self.q["v"].append(lambda e, sem=sem, tgt=tgt: e.wait_ge(sem, tgt))
        self.tags["v"].append(self.cur + "|VSW")
        return tgt


class _MockIns:
    def __init__(self, sim):
        self.sim = sim

    def then_inc(self, sem, n=1):
        self.sim.pending.append((id(sem), n))
        return self


class _MockEng:
    def __init__(self, sim):
        self.sim = sim

    def wait_ge(self, sem, val):
        self.sim.waits.append((id(sem), val))
        return _MockIns(self.sim)

    def __getattr__(self, name):
        return lambda *a, **k: _MockIns(self.sim)


def simulate(p, semnames):
    class S:
        pending, waits = [], []
    sim = S()
    sems, ptrs = {}, {k: 0 for k in p.q}
    engs = {k: _MockEng(sim) for k in p.q}
    progress = True
    while progress:
        progress = False
        for k in p.q:
            while ptrs[k] < len(p.q[k]):
                sim.waits, sim.pending = [], []
                p.q[k][ptrs[k]](engs[k])
                if not all(sems.get(s_, 0) >= v for s_, v in sim.waits):
                    break
                for s_, n in sim.pending:
                    sems[s_] = sems.get(s_, 0) + n
                ptrs[k] += 1
                progress = True
    stuck = {k: (ptrs[k], len(p.q[k])) for k in p.q if ptrs[k] < len(p.q[k])}
    if stuck:
        for k in stuck:
            around = [p.tags[k][i] if 0 <= i < len(p.tags[k]) else "?"
                      for i in range(ptrs[k] - 2, ptrs[k] + 3)]
            print(f"stuck {k}@{ptrs[k]}: tags around {around}")
        det = {}
        for k in stuck:
            sim.waits, sim.pending = [], []
            p.q[k][ptrs[k]](engs[k])
            det[k] = [(semnames.get(s_, "?"), v, sems.get(s_, 0)) for s_, v in sim.waits]
        raise RuntimeError(f"program deadlock: {stuck} {det}")


def build_nc(debug=False):
    nc = bass.Bass("TRN2", target_bir_lowering=False)
    dr = lambda n, shp, dt=F32, kind="ExternalInput": nc.dram_tensor(n, shp, dt, kind=kind).ap()
    x_in = dr("x_in", [L, D])
    wsw = dr("wsw", [GPT, 128, 2048])
    gw_in = dr("gw_in", [128, DC * 7])
    gbb_in = dr("gbb_in", [128, 8])
    bbb_in = dr("bbb_in", [128, 8])
    wq_in = dr("wq_in", [128, DC * HD])
    wk_in = dr("wk_in", [128, DC * HD])
    wv_in = dr("wv_in", [128, DC * HD])
    wo_in = dr("wo_in", [128, D])
    bqkv_in = dr("bqkv_in", [128, 4])
    bo_in = dr("bo_in", [128, DC])
    ones_in = dr("ones_in", [128, 128])
    onescol_in = dr("onescol_in", [128, 2])
    ident_in = dr("ident_in", [128, 128])
    masks_in = dr("masks_in", [128, 4 * T])
    y_out = dr("y_out", [L, D], kind="ExternalOutput")
    xf = dr("xf", [DC, 128, L], kind="Internal")
    h1 = dr("h1", [DC, 128, L], kind="Internal")
    h2 = dr("h2", [DC, 128, L], kind="Internal")
    wbf = dr("wbf", [GPT, 128, 2048], BF16, kind="Internal")
    if debug:
        dbg_h1 = dr("dbg_h1", [DC, 128, L], kind="ExternalOutput")
        dbg_h2 = dr("dbg_h2", [DC, 128, L], kind="ExternalOutput")
        dbg_xf = dr("dbg_xf", [DC, 128, L], kind="ExternalOutput")
        dbg_acc = dr("dbg_acc", [DC, 128, L], kind="ExternalOutput")
        dbg_e = dr("dbg_e", [E, DC, 128, T], kind="ExternalOutput")

    with contextlib.ExitStack() as ctx:
        sb = lambda n, shp, dt: ctx.enter_context(nc.sbuf_tensor(n, shp, dt))
        psa = lambda n, shp, dt=F32: ctx.enter_context(nc.psum_tensor(n, shp, dt))
        sem = lambda n: ctx.enter_context(nc.semaphore(n))

        wstream = sb("wstream", [128, NBUF * 2048], BF16)
        xn = sb("xn", [128, 2 * DC * T], BF16)
        hp = sb("hp", [128, DC * T], BF16)
        acc = sb("acc", [128, DC * T], F32)
        sch = sb("sch", [128, 2 * T], BF16)
        t1ch = sb("t1ch", [128, 2 * T], BF16)
        xch = sb("xch", [128, 4 * T], F32)
        x2ch = sb("x2ch", [128, 2 * T], F32)
        otb = sb("otb", [128, 2 * 2048], F32)
        rows = sb("rows", [1, 6 * T], F32)
        mub = sb("mub", [128, T], F32)
        rb = sb("rb", [128, T], F32)
        gwsb = sb("gwsb", [128, DC * 7], BF16)
        gbc = sb("gbc", [128, 8], F32)
        bbc = sb("bbc", [128, 8], F32)
        gsc = sb("gsc", [128, 16], F32)
        gt = sb("gt", [128, 8 * 7], F32)
        wTsb = sb("wTsb", [7, 4 * 128], F32)
        wbt = sb("wbt", [128, 2 * 8 * T], BF16)
        onescol = sb("onescol", [128, 2], F32)
        onesrow_b = sb("onesrow_b", [1, 128], BF16)
        onesrow_f = sb("onesrow_f", [1, 128], F32)
        ident = sb("ident", [128, 128], F32)
        identb = sb("identb", [128, 128], BF16)
        masks = sb("masks", [128, 4 * T], BF16)
        wqkv = sb("wqkv", [128, 3 * D], BF16)
        wo = sb("wo", [128, D], BF16)
        bqkv = sb("bqkv", [128, 4], F32)
        bo = sb("bo", [128, DC], F32)
        arow = sb("arow", [128, 8], F32)

        qsb = wstream[:, 0:L]
        ksb = wstream[:, L:2 * L]
        vT = wstream[:, 2 * L:3 * L]
        wrow = sb("wrow", [1, 7 * T], BF16)

        ps_rot = psa("ps_rot", [128, 4 * 512])
        ps_tr = psa("ps_tr", [128, 512])
        ps_trb = psa("ps_trb", [128, 1024], BF16)
        ps_wa = psa("ps_wa", [128, 512])
        ps_wb = psa("ps_wb", [128, 512])

        s_row = sem("s_row")
        s_mm, s_sc, s_ve = sem("s_mm"), sem("s_sc"), sem("s_ve")
        s_cst = sem("s_cst")
        s_acst = sem("s_acst")
        s_dbg = sem("s_dbg")
        s_racc = sem("s_racc")
        SC = [sem(f"s_xc{i}") for i in range(4)]
        SW = [sem(f"s_wl{i}") for i in range(NBUF)]
        STE = [sem(f"s_te{i}") for i in range(NBUF)]
        SO = [sem(f"s_ot{i}") for i in range(2)]
        SY = [sem(f"s_yo{i}") for i in range(2)]
        SX2 = [sem(f"s_x2{i}") for i in range(2)]

        p = Prog()
        c_row, c_mm, c_sc, c_ve = (Ctr() for _ in range(4))
        c_cst, c_acst, c_dbg, c_racc = Ctr(), Ctr(), Ctr(), Ctr()
        cSC = [Ctr() for _ in range(4)]
        cSW = [Ctr() for _ in range(NBUF)]
        cSTE = [Ctr() for _ in range(NBUF)]
        cSO = [Ctr() for _ in range(2)]
        cSY = [Ctr() for _ in range(2)]
        cSX2 = [Ctr() for _ in range(2)]
        p.vsem, p.vctr = s_ve, c_ve

        slot_free = [None] * 4
        slot_ctr = [0]

        def take_slot():
            sl = slot_ctr[0] % 4
            slot_ctr[0] += 1
            if slot_free[sl] is not None:
                p.wait("t", *slot_free[sl])
            return sl

        rot = lambda sl: ps_rot[:, sl * 512:(sl + 1) * 512]

        xnv = lambda b: xn[:, b * DC * T:(b + 1) * DC * T]
        wbtv = lambda b: wbt[:, b * 8 * T:(b + 1) * 8 * T]
        GT = {n: gt[:, i * 7:(i + 1) * 7] for i, n in
              enumerate(["log", "pp", "b", "eqm", "b2", "mask", "pg", "w05"])}
        COL = {n: gsc[:, i:i + 1] for i, n in enumerate(["m1", "m2", "m3", "sum", "r"])}
        ROWS = {n: rows[0:1, i * T:(i + 1) * T] for i, n in
                enumerate(["mu", "var", "sig", "rstd", "tmp", "tmp2"])}

        mm_of_group = {}
        wld_tgt, tee_tgt = {}, {}
        t1_of_par = {}
        xch_bar = {"mm": 0, "sc": 0, "ve": 0}

        for f in [
            lambda e: e.dma_start(out=onescol[:], in_=onescol_in),
            lambda e: e.dma_start(out=ident[:], in_=ident_in),
            lambda e: e.dma_start(out=gbc[:], in_=gbb_in),
            lambda e: e.dma_start(out=bbc[:], in_=bbb_in),
            lambda e: e.dma_start(out=bqkv[:], in_=bqkv_in),
            lambda e: e.dma_start(out=bo[:], in_=bo_in),
            lambda e: e.dma_start(out=onesrow_f[:], in_=ones_in[0:1, :]),
        ]:
            p.emit("y", lambda e, f=f: f(e).then_inc(s_cst, 16))
            c_cst.inc(16)
        for f in [
            lambda e: e.dma_start(out=masks[:], in_=masks_in),
            lambda e: e.dma_start(out=gwsb[:], in_=gw_in),
            lambda e: e.dma_start(out=onesrow_b[:], in_=ones_in[0:1, :]),
            lambda e: e.dma_start(out=identb[:], in_=ident_in),
            lambda e: e.dma_start(out=wqkv[:, 0:D], in_=wq_in),
            lambda e: e.dma_start(out=wqkv[:, D:2 * D], in_=wk_in),
            lambda e: e.dma_start(out=wqkv[:, 2 * D:3 * D], in_=wv_in),
            lambda e: e.dma_start(out=wo[:], in_=wo_in),
        ]:
            p.emit("g", lambda e, f=f: f(e).then_inc(s_cst, 16))
            c_cst.inc(16)
        consts_ld = c_cst.v
        p.vstep(lambda e: e.memset(wbt[:, 0:T], 0.5))
        ve_wbt0 = p.vstep(lambda e: e.memset(wbt[:, 8 * T:9 * T], 0.5))

        p.wait("t", s_cst, consts_ld)
        ent_mm, ent_ev, ent_st = {}, {}, {}
        acc_r = acc[:].rearrange("p (d t) -> p d t", d=DC)
        pst_r = ps_tr[:].rearrange("p (a t) -> p a t", a=4)
        pswa_r = ps_wa[:].rearrange("p (a t) -> p a t", a=4)
        for tile in range(NT):
            for tcit in range(4):
                tc = tile * 4 + tcit
                ob = tc % 2

                def lde(e, tc=tc, ob=ob):
                    if tc >= 2:
                        e.wait_ge(s_mm, ent_mm[tc - 2])
                    e.dma_start(out=otb[:, ob * 2048:(ob + 1) * 2048],
                                in_=x_in[tc * 128:(tc + 1) * 128, :]).then_inc(SO[ob], 16)
                p.emit("s", lde)
                cSO[ob].inc(16)
                p.wait("t", SO[ob], cSO[ob].v)
                for dcg in range(4):
                    tb, tb_r = (ps_tr, pst_r) if dcg % 2 == 0 else (ps_wa, pswa_r)
                    prev_key = (tc, dcg - 2) if dcg >= 2 else (tc - 1, dcg + 2)
                    if prev_key in ent_ev:
                        p.wait("t", s_ve, ent_ev[prev_key])
                    for j in range(4):
                        dc = dcg * 4 + j

                        def trf(e, ob=ob, dc=dc, j=j, tb=tb, fin=(j == 3)):
                            ins = e.transpose(tb[:, j * 128:(j + 1) * 128],
                                              otb[:, ob * 2048 + dc * 128: ob * 2048 + (dc + 1) * 128],
                                              ident[:])
                            if fin:
                                ins.then_inc(s_mm, 1)
                        p.emit("t", trf)
                    c_mm.inc()
                    ent_mm[tc] = c_mm.v
                    p.wait("v", s_mm, c_mm.v)
                    if tcit == 0 and dcg == 0 and tile >= 1:
                        p.wait("v", SX2[(tile - 1) % 2], ent_st[tile - 1])
                    ent_ev[(tc, dcg)] = p.vstep(lambda e, dcg=dcg, tcit=tcit, tb_r=tb_r: e.tensor_copy(
                        acc_r[:, dcg * 4:(dcg + 1) * 4, tcit * 128:(tcit + 1) * 128],
                        tb_r[:]))
            for dc in range(DC):
                def ste(e, tile=tile, dc=dc, tgt=ent_ev[(tile * 4 + 3, 3)]):
                    if dc == 0:
                        e.wait_ge(s_ve, tgt)
                    e.dma_start(out=xf[dc][:, tile * T:(tile + 1) * T],
                                in_=acc[:, dc * T:(dc + 1) * T]).then_inc(SX2[tile % 2], 16)
                p.emit("y", ste)
                cSX2[tile % 2].inc(16)
            ent_st[tile] = cSX2[tile % 2].v

        def ln_tile(src, tile, xb):
            mm_now, ve_now = c_mm.v, c_ve.v
            x2w = [cSX2[0].v, cSX2[1].v]
            mm_stats, sq_at = {}, {}
            sl_a, sl_b = take_slot(), take_slot()
            for dc in range(DC):
                b4, b2 = dc % 4, dc % 2

                def ldx(e, dc=dc, b4=b4, bar=dict(xch_bar)):
                    if dc == 0:
                        if bar["mm"]:
                            e.wait_ge(s_mm, bar["mm"])
                        if bar["ve"]:
                            e.wait_ge(s_ve, bar["ve"])
                        if bar["sc"]:
                            e.wait_ge(s_sc, bar["sc"])
                    if dc >= 4:
                        e.wait_ge(s_mm, mm_stats[dc - 4])
                        e.wait_ge(s_sc, sq_at[dc - 4])
                    e.dma_start(out=xch[:, b4 * T:(b4 + 1) * T],
                                in_=src[dc][:, tile * T:(tile + 1) * T]).then_inc(SC[b4], 16)
                p.emit("y", ldx)
                cSC[b4].inc(16)
                ld_tgt = cSC[b4].v
                if dc == 0:
                    p.wait("s", s_ve, ve_now)
                    p.wait("s", SX2[0], x2w[0])
                    p.wait("s", SX2[1], x2w[1])
                p.wait("s", SC[b4], ld_tgt)
                if dc >= 2:
                    p.wait("s", s_mm, mm_stats[dc - 2])
                p.emit("s", lambda e, b4=b4, b2=b2: e.activation(
                    x2ch[:, b2 * T:(b2 + 1) * T], xch[:, b4 * T:(b4 + 1) * T],
                    AF.Square).then_inc(s_sc, 1))
                c_sc.inc()
                sq_at[dc] = c_sc.v
                p.wait("t", SC[b4], ld_tgt)
                p.emit("t", lambda e, b4=b4, dc=dc, sl_a=sl_a: e.matmul(
                    rot(sl_a)[0:1, :], onescol[:, 0:1], xch[:, b4 * T:(b4 + 1) * T],
                    start=(dc == 0), stop=(dc == DC - 1)))
                p.wait("t", s_sc, c_sc.v)
                p.emit("t", lambda e, b2=b2, dc=dc, sl_b=sl_b: e.matmul(
                    rot(sl_b)[0:1, :], onescol[:, 0:1], x2ch[:, b2 * T:(b2 + 1) * T],
                    start=(dc == 0), stop=(dc == DC - 1)).then_inc(s_mm, 1))
                c_mm.inc()
                mm_stats[dc] = c_mm.v
            p.wait("v", s_mm, c_mm.v)
            p.vstep(lambda e: e.tensor_scalar(ROWS["mu"], rot(sl_a)[0:1, :],
                                              1.0 / D, None, op0=ALU.mult))
            p.vstep(lambda e: e.tensor_scalar(ROWS["var"], rot(sl_b)[0:1, :],
                                              1.0 / D, 1e-5, op0=ALU.mult, op1=ALU.add))
            p.vstep(lambda e: e.tensor_tensor(ROWS["tmp"], ROWS["mu"], ROWS["mu"], op=ALU.mult))
            tgt = p.vstep(lambda e: e.tensor_tensor(ROWS["var"], ROWS["var"], ROWS["tmp"],
                                                    op=ALU.subtract))
            slot_free[sl_a] = slot_free[sl_b] = (s_ve, tgt)
            p.wait("s", s_ve, tgt)
            p.emit("s", lambda e: e.activation(ROWS["sig"], ROWS["var"], AF.Sqrt).then_inc(s_sc, 1))
            c_sc.inc()
            p.wait("v", s_sc, c_sc.v)
            ve_r = p.vstep(lambda e: e.reciprocal(ROWS["rstd"], ROWS["sig"]))
            sl_c, sl_d = take_slot(), take_slot()
            p.wait("t", s_ve, ve_r)
            p.emit("t", lambda e: e.matmul(rot(sl_c), onesrow_f[:], ROWS["mu"],
                                           start=True, stop=True))
            p.emit("t", lambda e: e.matmul(rot(sl_d), onesrow_f[:], ROWS["rstd"],
                                           start=True, stop=True).then_inc(s_mm, 1))
            c_mm.inc()
            p.wait("v", s_mm, c_mm.v)
            p.vstep(lambda e: e.tensor_copy(mub[:], rot(sl_c)))
            ve_bc = p.vstep(lambda e: e.tensor_copy(rb[:], rot(sl_d)))
            slot_free[sl_c] = slot_free[sl_d] = (s_ve, ve_bc)
            norm_at = {}
            for dc in range(DC):
                b4 = dc % 4

                def ldx2(e, dc=dc, b4=b4):
                    e.wait_ge(s_ve, ve_bc if dc < 4 else norm_at[dc - 4])
                    e.dma_start(out=xch[:, b4 * T:(b4 + 1) * T],
                                in_=src[dc][:, tile * T:(tile + 1) * T]).then_inc(SC[b4], 16)
                p.emit("y", ldx2)
                cSC[b4].inc(16)
                p.wait("v", SC[b4], cSC[b4].v)
                p.vstep(lambda e, b4=b4: e.tensor_tensor(
                    x2ch[:, 0:T], xch[:, b4 * T:(b4 + 1) * T], mub[:], op=ALU.subtract))
                norm_at[dc] = p.vstep(lambda e, dc=dc, xb=xb: e.tensor_tensor(
                    xnv(xb)[:, dc * T:(dc + 1) * T], x2ch[:, 0:T], rb[:], op=ALU.mult))
            xch_bar["mm"] = mm_stats[DC - 1]
            xch_bar["sc"] = sq_at[DC - 1]
            xch_bar["ve"] = norm_at[DC - 1]
            return norm_at[DC - 1]

        def gating_tile(ve_xn, xb):
            mm_now, ve_now = c_mm.v, c_ve.v
            gate_done, logtt, wb_at = {}, {}, {}
            for tc in range(4):
                sl = take_slot()
                p.wait("t", s_ve, ve_xn if tc == 0 else logtt[tc - 1])
                for dc in range(DC):
                    p.emit("t", lambda e, dc=dc, tc=tc, sl=sl, xb=xb: e.matmul(
                        rot(sl)[:, 0:7], xnv(xb)[:, dc * T + tc * 128: dc * T + (tc + 1) * 128],
                        gwsb[:, dc * 7:(dc + 1) * 7], start=(dc == 0), stop=(dc == DC - 1)))
                p.emit("t", lambda e, sl=sl, xb=xb: e.matmul(
                    rot(sl)[:, 8:512], xnv(xb)[:, 0:128], xnv(xb)[:, 0:504],
                    start=False, stop=True).then_inc(s_mm, 1))
                c_mm.inc()
                p.wait("v", s_mm, c_mm.v)
                p.vstep(lambda e, sl=sl: e.tensor_tensor(GT["log"], rot(sl)[:, 0:7],
                                                         gbc[:, 0:7], op=ALU.add))
                logtt[tc] = c_ve.v
                slot_free[sl] = (s_ve, logtt[tc])
                p.vstep(lambda e: e.reduce_max(COL["m1"], GT["log"], axis=AXX))
                p.vstep(lambda e: e.tensor_scalar(GT["pp"], GT["log"], COL["m1"], None,
                                                  op0=ALU.subtract))
                p.wait("s", s_ve, c_ve.v)
                p.emit("s", lambda e: e.activation(GT["pp"], GT["pp"], AF.Exp,
                                                   accum_out=COL["sum"]).then_inc(s_sc, 1))
                c_sc.inc()
                p.wait("v", s_sc, c_sc.v)
                for f in [
                    lambda e: e.reciprocal(COL["r"], COL["sum"]),
                    lambda e: e.tensor_scalar(GT["pp"], GT["pp"], COL["r"], None, op0=ALU.mult),
                    lambda e: e.tensor_tensor(GT["b"], GT["pp"], bbc[:, 0:7], op=ALU.add),
                    lambda e: e.reduce_max(COL["m1"], GT["b"], axis=AXX),
                    lambda e: e.tensor_scalar(GT["eqm"], GT["b"], COL["m1"], NEG,
                                              op0=ALU.is_ge, op1=ALU.mult),
                    lambda e: e.tensor_tensor(GT["b2"], GT["b"], GT["eqm"], op=ALU.add),
                    lambda e: e.reduce_max(COL["m2"], GT["b2"], axis=AXX),
                    lambda e: e.tensor_scalar(GT["eqm"], GT["b2"], COL["m2"], NEG,
                                              op0=ALU.is_ge, op1=ALU.mult),
                    lambda e: e.tensor_tensor(GT["b2"], GT["b2"], GT["eqm"], op=ALU.add),
                    lambda e: e.reduce_max(COL["m3"], GT["b2"], axis=AXX),
                    lambda e: e.tensor_scalar(GT["mask"], GT["b"], COL["m3"], None, op0=ALU.is_ge),
                    lambda e: e.tensor_tensor(GT["pg"], GT["pp"], GT["mask"], op=ALU.mult),
                    lambda e: e.reduce_sum(COL["sum"], GT["pg"], axis=AXX),
                    lambda e: e.reciprocal(COL["r"], COL["sum"]),
                ]:
                    p.vstep(f)
                ve_w = p.vstep(lambda e: e.tensor_scalar(GT["w05"], GT["pg"], COL["r"], 0.5,
                                                         op0=ALU.mult, op1=ALU.mult))
                p.wait("t", s_ve, ve_w)
                p.emit("t", lambda e: e.transpose(ps_tr[0:7, 0:128], GT["w05"],
                                                  ident[:]).then_inc(s_mm, 1))
                c_mm.inc()
                p.wait("v", s_mm, c_mm.v)
                gate_done[tc] = p.vstep(lambda e, tc=tc: e.tensor_copy(
                    wTsb[:, tc * 128:(tc + 1) * 128], ps_tr[0:7, 0:128]))
            ve_gate = gate_done[3]
            for ex in range(7):
                def rowdma(e, ex=ex):
                    if ex == 0:
                        e.wait_ge(s_ve, ve_gate)
                        e.wait_ge(s_mm, mm_now)
                    e.dma_start(out=wrow[0:1, ex * T:(ex + 1) * T],
                                in_=wTsb[ex:ex + 1, :]).then_inc(s_row, 16)
                p.emit("g", rowdma)
                c_row.inc(16)
            p.wait("t", s_row, c_row.v)
            p.wait("t", s_ve, ve_gate)
            for ex in range(7):
                sl = take_slot()
                p.emit("t", lambda e, ex=ex, sl=sl: e.matmul(
                    rot(sl), onesrow_b[:], wrow[0:1, ex * T:(ex + 1) * T],
                    start=True, stop=True).then_inc(s_mm, 1))
                c_mm.inc()
                p.wait("s", s_mm, c_mm.v)
                if ex == 0:
                    p.wait("s", s_ve, ve_now)
                p.emit("s", lambda e, ex=ex, sl=sl, xb=xb: e.copy(
                    wbtv(xb)[:, (ex + 1) * T:(ex + 2) * T], rot(sl)).then_inc(s_sc, 1))
                c_sc.inc()
                wb_at[ex] = c_sc.v
                slot_free[sl] = (s_sc, c_sc.v)
            return wb_at

        def ffn_tile(call, tile, ve_xn, wb_at, prev_acc_reader, xb, hooks=None):
            yev = {}
            first_pass = (call == 0 and tile == 0)
            g0 = call * NT * GPT + tile * GPT

            mm_before = c_mm.v if (call == 1 and tile == 0) else None

            def emit_prefetch_range(k0, k1):
              for k in range(k0, k1):
                gg = g0 + k

                b = gg % NBUF
                tee_barrier = cSTE[:] if (mm_before is not None and k == 0) else None
                tee_barrier_v = [c.v for c in cSTE] if tee_barrier else None

                def pf(e, gg=gg, k=k, b=b, tee_barrier_v=tee_barrier_v):
                    if mm_before is not None and k == 0:
                        e.wait_ge(s_mm, mm_before)
                        for bi in range(NBUF):
                            if tee_barrier_v[bi] > 0:
                                e.wait_ge(STE[bi], tee_barrier_v[bi])
                    if gg >= DEPTH:
                        e.wait_ge(s_mm, mm_of_group[gg - DEPTH])
                    src_blk = wsw[gg % GPT] if first_pass else wbf[gg % GPT]
                    kw = dict(max_dma_last_dim=4096) if first_pass else {}
                    if first_pass and gg >= NBUF:
                        e.wait_ge(STE[b], ((gg - NBUF) // NBUF + 1) * 16)
                    if (not first_pass) and call == 0:
                        e.wait_ge(STE[b], tee_tgt[gg % GPT])
                    e.dma_start(out=wstream[:, b * 2048:(b + 1) * 2048],
                                in_=src_blk, **kw).then_inc(SW[b], 16)
                p.emit("g", pf)
                cSW[b].inc(16)
                wld_tgt[gg] = cSW[b].v

            def mm_group(gg, rhs_fn, extra=()):
                sl = take_slot()
                p.wait("t", SW[gg % NBUF], wld_tgt[gg])
                for w in extra:
                    p.wait("t", *w)
                buf = gg % NBUF
                if first_pass:
                    def tee(e, gg=gg, b=buf, wt=wld_tgt[gg]):
                        e.wait_ge(SW[b], wt)
                        e.dma_start(out=wbf[gg],
                                    in_=wstream[:, b * 2048:(b + 1) * 2048]
                                    ).then_inc(STE[b], 16)
                    p.emit("s", tee)
                    cSTE[buf].inc(16)
                    tee_tgt[gg] = cSTE[buf].v
                for k in range(DC):
                    def mmf(e, sl=sl, buf=buf, k=k, rhs_fn=rhs_fn):
                        ins = e.matmul(rot(sl),
                                       wstream[:, buf * 2048 + k * 128: buf * 2048 + (k + 1) * 128],
                                       rhs_fn(k), start=(k == 0), stop=(k == DC - 1))
                        if k == DC - 1:
                            ins.then_inc(s_mm, 1)
                    p.emit("t", mmf)
                c_mm.inc()
                mm_of_group[gg] = c_mm.v
                return sl

            gg = g0
            for e_ in range(E):
                if hooks and e_ in hooks:
                    hooks[e_]()
                emit_prefetch_range(e_ * 48, (e_ + 1) * 48)
                for hc in range(DC):
                    sl = mm_group(gg, lambda k, xb=xb: xnv(xb)[:, k * T:(k + 1) * T])
                    p.wait("s", s_mm, c_mm.v)
                    if t1_of_par.get(hc % 2):
                        p.wait("s", s_ve, t1_of_par[hc % 2])
                    p.emit("s", lambda e, sl=sl, hc=hc: e.activation(
                        sch[:, (hc % 2) * T:(hc % 2 + 1) * T], rot(sl),
                        AF.Silu).then_inc(s_sc, 1))
                    c_sc.inc()
                    slot_free[sl] = (s_sc, c_sc.v)
                    sc_silu = c_sc.v
                    gg += 1
                    sl = mm_group(gg, lambda k, xb=xb: xnv(xb)[:, k * T:(k + 1) * T])
                    p.wait("v", s_mm, c_mm.v)
                    p.wait("v", s_sc, sc_silu)
                    tgt_t1 = p.vstep(lambda e, sl=sl, hc=hc: e.tensor_tensor(
                        t1ch[:, (hc % 2) * T:(hc % 2 + 1) * T], rot(sl),
                        sch[:, (hc % 2) * T:(hc % 2 + 1) * T], op=ALU.mult))
                    slot_free[sl] = (s_ve, tgt_t1)
                    t1_of_par[hc % 2] = tgt_t1
                    if e_ > 0 and hc == 0:
                        p.wait("v", s_sc, wb_at[e_ - 1])
                    if first_pass and e_ == 0 and hc == 0:
                        p.wait("v", s_ve, ve_wbt0)
                    p.vstep(lambda e, hc=hc, e_=e_, xb=xb: e.tensor_tensor(
                        hp[:, hc * T:(hc + 1) * T], t1ch[:, (hc % 2) * T:(hc % 2 + 1) * T],
                        wbtv(xb)[:, e_ * T:(e_ + 1) * T], op=ALU.mult))
                    gg += 1
                hp_ready = c_ve.v
                for dc in range(DC):
                    extra = [(s_ve, hp_ready)] if dc == 0 else []
                    if e_ == 0 and dc == 0 and prev_acc_reader is not None:
                        extra.append(prev_acc_reader)
                    sl = mm_group(gg, lambda k: hp[:, k * T:(k + 1) * T], extra)
                    p.wait("v", s_mm, c_mm.v)
                    if e_ == 0:
                        tgt = p.vstep(lambda e, sl=sl, dc=dc: e.tensor_copy(
                            acc[:, dc * T:(dc + 1) * T], rot(sl)))
                    else:
                        tgt = p.vstep(lambda e, sl=sl, dc=dc: e.tensor_tensor(
                            acc[:, dc * T:(dc + 1) * T], rot(sl), acc[:, dc * T:(dc + 1) * T],
                            op=ALU.add))
                    slot_free[sl] = (s_ve, tgt)
                    if e_ == E - 1:
                        yev[dc] = tgt
                    gg += 1
                if debug and call == 1 and tile == 1:
                    for dc in range(DC):
                        def dste(e, e_=e_, dc=dc, tgt=tgt):
                            if dc == 0:
                                e.wait_ge(s_ve, tgt)
                            e.dma_start(out=dbg_e[e_][dc],
                                        in_=acc[:, dc * T:(dc + 1) * T]).then_inc(s_dbg, 16)
                        p.emit("y", dste)
                        c_dbg.inc(16)
            return yev

        def residual_add_tile(tile, src, yev):
            bar = dict(xch_bar)
            add_at = {}
            for dc in range(DC):
                b4 = dc % 4

                def ldr(e, dc=dc, b4=b4, tile=tile, bar=bar):
                    if dc == 0:
                        if bar["mm"]:
                            e.wait_ge(s_mm, bar["mm"])
                        if bar["ve"]:
                            e.wait_ge(s_ve, bar["ve"])
                        if bar["sc"]:
                            e.wait_ge(s_sc, bar["sc"])
                    if dc >= 4:
                        e.wait_ge(s_ve, add_at[dc - 4])
                    e.dma_start(out=xch[:, b4 * T:(b4 + 1) * T],
                                in_=src[dc][:, tile * T:(tile + 1) * T]).then_inc(SC[b4], 16)
                p.emit("y", ldr)
                cSC[b4].inc(16)
                p.wait("v", SC[b4], cSC[b4].v)
                p.wait("v", s_ve, yev[dc])
                add_at[dc] = p.vstep(lambda e, dc=dc, b4=b4: e.tensor_tensor(
                    acc[:, dc * T:(dc + 1) * T], acc[:, dc * T:(dc + 1) * T],
                    xch[:, b4 * T:(b4 + 1) * T], op=ALU.add))
            xch_bar["ve"] = add_at[DC - 1]
            return add_at

        def residual_store_tile(tile, src, dst, yev):
            add_at = residual_add_tile(tile, src, yev)
            for dc in range(DC):
                def stx(e, dc=dc, tile=tile, tgt=add_at[dc]):
                    e.wait_ge(s_ve, tgt)
                    e.dma_start(out=dst[dc][:, tile * T:(tile + 1) * T],
                                in_=acc[:, dc * T:(dc + 1) * T]).then_inc(s_acst, 16)
                p.emit("y", stx)
                c_acst.inc(16)
            return (s_acst, c_acst.v)

        exit_state = {"pe_prev": {0: None, 1: None}, "st": {}}

        def residual_exit_tile(tile, src, yev):
            add_at = residual_add_tile(tile, src, yev)
            p.wait("t", s_ve, add_at[DC - 1])
            if debug:
                for dc in range(DC):
                    def dstx(e, dc=dc, tile=tile, tgt=res_at[dc]):
                        e.wait_ge(s_ve, tgt)
                        e.dma_start(out=dbg_acc[dc][:, tile * T:(tile + 1) * T],
                                    in_=acc[:, dc * T:(dc + 1) * T]).then_inc(s_st, 16)
                    p.emit("y", dstx)
                    c_st.inc(16)
            for tc in range(4):
                ob = tc % 2
                ev_at = {}
                for dcg in range(4):
                    par = dcg % 2
                    tb = ps_tr if par == 0 else ps_wa
                    prev = ev_at.get(dcg - 2, exit_state["pe_prev"][par])
                    if prev is not None:
                        p.wait("t", s_ve, prev)
                    for j in range(4):
                        dc = dcg * 4 + j

                        def trf(e, dc=dc, j=j, tc=tc, tb=tb, fin=(j == 3)):
                            ins = e.transpose(
                                tb[:, j * 128:(j + 1) * 128],
                                acc[:, dc * T + tc * 128: dc * T + (tc + 1) * 128], ident[:])
                            if fin:
                                ins.then_inc(s_mm, 1)
                        p.emit("t", trf)
                    c_mm.inc()
                    p.wait("v", s_mm, c_mm.v)
                    key = (tile, tc - 2) if tc >= 2 else (tile - 1, tc + 2)
                    if dcg == 0 and key in exit_state["st"]:
                        p.wait("v", SY[ob], exit_state["st"][key])
                    ev_at[dcg] = p.vstep(lambda e, ob=ob, dcg=dcg, tb=tb: e.tensor_copy(
                        otb[:, ob * 2048 + dcg * 512: ob * 2048 + (dcg + 1) * 512], tb[:]))
                exit_state["pe_prev"][0] = ev_at[2]
                exit_state["pe_prev"][1] = ev_at[3]

                def sty(e, tile=tile, tc=tc, ob=ob, tgt=ev_at[3]):
                    e.wait_ge(s_ve, tgt)
                    e.dma_start(out=y_out[tile * T + tc * 128: tile * T + (tc + 1) * 128, :],
                                in_=otb[:, ob * 2048:(ob + 1) * 2048]).then_inc(SY[ob], 16)
                p.emit("y", sty)
                cSY[ob].inc(16)
                exit_state["st"][(tile, tc)] = cSY[ob].v
            return (s_mm, c_mm.v)

        def attention(src, dst, tail_hook=None, pre_ve0=None):
            qkv_done, pv_ev, wo_st = {}, {}, {}
            ve_xn_of = {}
            if pre_ve0 is not None:
                ve_xn_of[0] = pre_ve0

            def attn_ln(sl_):
                p.mark(f"attn.ln{sl_}")
                ve_xn_of[sl_] = ln_tile(src, sl_, sl_ % 2)

            def attn_qkv(sl_):
                p.mark(f"attn.qkv{sl_}")
                for m in range(3):
                    sl = take_slot()
                    if m == 0:
                        p.wait("t", s_ve, ve_xn_of[sl_])
                    for dc in range(DC):
                        def mmq(e, sl=sl, m=m, dc=dc, sl_=sl_):
                            ins = e.matmul(rot(sl),
                                           wqkv[:, m * D + dc * 128: m * D + (dc + 1) * 128],
                                           xnv(sl_ % 2)[:, dc * T:(dc + 1) * T],
                                           start=(dc == 0), stop=(dc == DC - 1))
                            if dc == DC - 1:
                                ins.then_inc(s_mm, 1)
                        p.emit("t", mmq)
                    c_mm.inc()
                    p.wait("v", s_mm, c_mm.v)
                    if m == 0:
                        tgt = p.vstep(lambda e, sl=sl, sl_=sl_: e.tensor_scalar(
                            qsb[:, sl_ * T:(sl_ + 1) * T], rot(sl), bqkv[:, 0:1], ISQ,
                            op0=ALU.add, op1=ALU.mult))
                    elif m == 1:
                        tgt = p.vstep(lambda e, sl=sl, sl_=sl_: e.tensor_scalar(
                            ksb[:, sl_ * T:(sl_ + 1) * T], rot(sl), bqkv[:, 1:2], None, op0=ALU.add))
                    else:
                        tgt = p.vstep(lambda e, sl=sl, sl_=sl_: e.tensor_scalar(
                            sch[:, 0:T], rot(sl), bqkv[:, 2:3], None, op0=ALU.add))
                    slot_free[sl] = (s_ve, tgt)
                p.wait("t", s_ve, c_ve.v)
                for j in range(4):
                    def trv(e, j=j, fin=(j == 3)):
                        ins = e.transpose(ps_trb[:, j * 128:(j + 1) * 128],
                                          sch[:, j * 128:(j + 1) * 128], identb[:])
                        if fin:
                            ins.then_inc(s_mm, 1)
                    p.emit("t", trv)
                c_mm.inc()
                p.wait("v", s_mm, c_mm.v)
                qkv_done[sl_] = p.vstep(lambda e, sl_=sl_: e.tensor_copy(
                    vT[:, sl_ * T:(sl_ + 1) * T], ps_trb[:, 0:512]))

            trB_done = {}

            def attn_rows_A(s_, qi):
                half = qi % 2
                ab = half * 3
                sneg = acc[:, half * 4096: half * 4096 + 4096]
                nks = s_ + 1
                qc = s_ * 4 + qi
                last_tgt = None
                first_ts = True
                for ks in range(nks):
                    sl = take_slot()
                    p.emit("t", lambda e, sl=sl, qc=qc, ks=ks: e.matmul(
                        rot(sl), qsb[:, qc * 128:(qc + 1) * 128],
                        ksb[:, ks * T:(ks + 1) * T], start=True,
                        stop=True).then_inc(s_mm, 1))
                    c_mm.inc()
                    p.wait("v", s_mm, c_mm.v)
                    if first_ts and trB_done.get(half) is not None:
                        p.wait("v", s_mm, trB_done[half])
                    first_ts = False
                    first = ks == 0
                    if ks < nks - 1:
                        last_tgt = p.vstep(lambda e, sl=sl, ks=ks, first=first, sneg=sneg, ab=ab:
                                           e.tensor_scalar(
                            sneg[:, ks * T:(ks + 1) * T], rot(sl), -1.0,
                            None if first else arow[:, ab:ab + 1],
                            op0=ALU.mult, op1=ALU.min, accum_out=arow[:, ab:ab + 1]))
                    else:
                        p.vstep(lambda e, sl=sl, qi=qi: e.tensor_tensor(
                            x2ch[:, 0:T], rot(sl), masks[:, qi * T:(qi + 1) * T], op=ALU.add))
                        last_tgt = p.vstep(lambda e, ks=ks, first=first, sneg=sneg, ab=ab:
                                           e.tensor_scalar(
                            sneg[:, ks * T:(ks + 1) * T], x2ch[:, 0:T], -1.0,
                            None if first else arow[:, ab:ab + 1],
                            op0=ALU.mult, op1=ALU.min, accum_out=arow[:, ab:ab + 1]))
                    slot_free[sl] = (s_ve, last_tgt)
                p.wait("s", s_ve, last_tgt)
                p.emit("s", lambda e, nks=nks, sneg=sneg, ab=ab: e.activation(
                    sneg[:, 0:nks * T], sneg[:, 0:nks * T], AF.Exp, scale=-1.0,
                    bias=arow[:, ab:ab + 1], accum_out=arow[:, ab + 1:ab + 2]).then_inc(s_sc, 1))
                c_sc.inc()
                p.wait("v", s_sc, c_sc.v)
                p.vstep(lambda e, ab=ab: e.reciprocal(arow[:, ab + 2:ab + 3], arow[:, ab + 1:ab + 2]))
                ve_p = p.vstep(lambda e, nks=nks, sneg=sneg, ab=ab: e.tensor_scalar(
                    sneg[:, 0:nks * T], sneg[:, 0:nks * T], arow[:, ab + 2:ab + 3],
                    None, op0=ALU.mult))
                return ve_p

            def attn_rows_B(s_, qi, ve_p):
                half = qi % 2
                sneg = acc[:, half * 4096: half * 4096 + 4096]
                nks = s_ + 1
                nkc = nks * 4
                p.wait("t", s_ve, ve_p)
                ptr_ev = {}
                for kc in range(nkc):
                    jj = kc % 4
                    if jj == 0 and kc >= 4:
                        p.wait("t", s_ve, ptr_ev[kc - 4])
                    fin = (jj == 3 or kc == nkc - 1)

                    def trp(e, kc=kc, jj=jj, fin=fin, sneg=sneg):
                        ins = e.transpose(ps_tr[:, jj * 128:(jj + 1) * 128],
                                          sneg[:, kc * 128:(kc + 1) * 128], ident[:])
                        if fin:
                            ins.then_inc(s_mm, 1)
                    p.emit("t", trp)
                    if fin:
                        c_mm.inc()
                        p.wait("v", s_mm, c_mm.v)
                        lo = (kc // 4) * 4
                        width = (kc - lo + 1) * 128
                        ptr_ev[lo] = p.vstep(lambda e, lo=lo, width=width: e.tensor_copy(
                            hp[:, lo * 128: lo * 128 + width], ps_tr[:, 0:width]))
                trB_done[half] = c_mm.v
                ve_ptb = ptr_ev[(nkc - 1) // 4 * 4]
                pv_ps = ps_wa if qi % 2 == 0 else ps_wb
                p.wait("t", s_ve, ve_ptb)
                if pv_ev.get(qi % 2):
                    p.wait("t", s_ve, pv_ev[qi % 2])
                p.wait("t", s_ve, qkv_done[s_])
                for kc in range(nkc):
                    def mpv(e, kc=kc, pv_ps=pv_ps, nkc=nkc):
                        ins = e.matmul(pv_ps[:, 0:128], vT[:, kc * 128:(kc + 1) * 128],
                                       hp[:, kc * 128:(kc + 1) * 128],
                                       start=(kc == 0), stop=(kc == nkc - 1))
                        if kc == nkc - 1:
                            ins.then_inc(s_mm, 1)
                    p.emit("t", mpv)
                c_mm.inc()
                trB_done[half] = c_mm.v
                p.wait("v", s_mm, c_mm.v)
                pv_ev[qi % 2] = p.vstep(lambda e, pv_ps=pv_ps, qi=qi: e.tensor_copy(
                    sch[:, T + qi * 128: T + (qi + 1) * 128], pv_ps[:, 0:128]))

            def attn_rows_wo(s_):
                p.mark(f"attn.rows{s_}")
                p.wait("v", SX2[0], cSX2[0].v)
                p.wait("v", SX2[1], cSX2[1].v)
                vp = {}
                vp[0] = attn_rows_A(s_, 0)
                vp[1] = attn_rows_A(s_, 1)
                attn_rows_B(s_, 0, vp[0])
                vp[2] = attn_rows_A(s_, 2)
                attn_rows_B(s_, 1, vp[1])
                vp[3] = attn_rows_A(s_, 3)
                attn_rows_B(s_, 2, vp[2])
                attn_rows_B(s_, 3, vp[3])
                attnF = sch[:, T:2 * T]
                p.mark(f"attn.wo{s_}")
                for dc in range(DC):
                    sl = take_slot()
                    if dc == 0:
                        p.wait("t", s_ve, pv_ev[1])
                    p.emit("t", lambda e, sl=sl, dc=dc: e.matmul(
                        rot(sl), wo[:, dc * 128:(dc + 1) * 128], attnF,
                        start=True, stop=True).then_inc(s_mm, 1))
                    c_mm.inc()
                    p.wait("v", s_mm, c_mm.v)
                    if dc >= 2:
                        p.wait("v", SX2[dc % 2], wo_st[dc - 2])
                    elif wo_st.get(dc + 14):
                        p.wait("v", SX2[dc % 2], wo_st[dc + 14])
                    tgt = p.vstep(lambda e, sl=sl, dc=dc: e.tensor_scalar(
                        x2ch[:, (dc % 2) * T:(dc % 2 + 1) * T], rot(sl), bo[:, dc:dc + 1],
                        None, op0=ALU.add))
                    slot_free[sl] = (s_ve, tgt)

                    def rac(e, dc=dc, s=s_, tgt=tgt):
                        e.wait_ge(s_ve, tgt)
                        e.dma_start(out=x2ch[:, (dc % 2) * T:(dc % 2 + 1) * T],
                                    in_=src[dc][:, s * T:(s + 1) * T],
                                    accum_op=ALU.add).then_inc(s_racc, 16)
                    p.emit("g", rac)
                    c_racc.inc(16)

                    def sth(e, dc=dc, s=s_, tgt=c_racc.v):
                        e.wait_ge(s_racc, tgt)
                        e.dma_start(out=dst[dc][:, s * T:(s + 1) * T],
                                    in_=x2ch[:, (dc % 2) * T:(dc % 2 + 1) * T]).then_inc(SX2[dc % 2], 16)
                    p.emit("y", sth)
                    cSX2[dc % 2].inc(16)
                    wo_st[dc] = cSX2[dc % 2].v

            if 0 not in ve_xn_of:
                attn_ln(0)
            attn_qkv(0)
            for s_ in range(NT):
                if s_ + 1 < NT:
                    attn_ln(s_ + 1)
                if s_ == NT - 1 and tail_hook is not None:
                    tail_hook()
                attn_rows_wo(s_)
                if s_ + 1 < NT:
                    attn_qkv(s_ + 1)

        XF = [xf[dc] for dc in range(DC)]
        H1 = [h1[dc] for dc in range(DC)]
        H2 = [h2[dc] for dc in range(DC)]

        def run_ffn_call(call, src, dst, prev, pre=None, hooks_last=None):
            nxt = {}
            if pre is None:
                p.mark(f"c{call}.ln0")
                ve_xn = ln_tile(src, 0, 0)
                wb_at = gating_tile(ve_xn, 0)
            else:
                ve_xn, wb_at = pre
            for tile in range(NT):
                xb = tile % 2
                first_pass = (call == 0 and tile == 0)
                hooks = None
                if tile + 1 < NT and not first_pass:
                    nb = (tile + 1) % 2
                    def mk_ln(tile=tile, nb=nb):
                        p.mark(f"c{call}.pln{tile + 1}")
                        nxt["ve"] = ln_tile(src, tile + 1, nb)
                    def mk_gate(nb=nb):
                        nxt["wb"] = gating_tile(nxt["ve"], nb)
                    hooks = {2: mk_ln, 4: mk_gate}
                elif tile == NT - 1 and hooks_last is not None:
                    hooks = hooks_last
                p.mark(f"c{call}.exp{tile}")
                yev = ffn_tile(call, tile, ve_xn, wb_at, prev, xb, hooks)
                p.mark(f"c{call}.res{tile}")
                if dst is not None:
                    prev = residual_store_tile(tile, src, dst, yev)
                else:
                    prev = residual_exit_tile(tile, src, yev)
                if tile + 1 < NT:
                    if first_pass:
                        nxt["ve"] = ln_tile(src, tile + 1, (tile + 1) % 2)
                        nxt["wb"] = gating_tile(nxt["ve"], (tile + 1) % 2)
                    ve_xn, wb_at = nxt["ve"], nxt["wb"]
            return prev

        attn_pre = {}

        def mk_attn_ln0():
            p.mark("attn.pln0")
            attn_pre["ve0"] = ln_tile(H1, 0, 0)
        run_ffn_call(0, XF, H1, None, hooks_last={2: mk_attn_ln0})
        p.mark("attn")
        f2pre = {}

        def tail():
            p.mark("c1.ln0")
            ve0 = ln_tile(H2, 0, 0)
            f2pre["pre"] = (ve0, gating_tile(ve0, 0))
        attention(H1, H2, tail_hook=tail, pre_ve0=attn_pre["ve0"])
        run_ffn_call(1, H2, None, (s_sc, c_sc.v), pre=f2pre["pre"])

        if debug:
            def dmp(e):
                e.dma_start(out=dbg_xf, in_=xf).then_inc(s_dbg, 16)
                e.dma_start(out=dbg_h1, in_=h1).then_inc(s_dbg, 16)
                e.dma_start(out=dbg_h2, in_=h2).then_inc(s_dbg, 16)
            p.emit("y", dmp)
            c_dbg.inc(48)
            p.wait("y", s_dbg, c_dbg.v)
        p.wait("y", SY[0], cSY[0].v)
        p.wait("y", SY[1], cSY[1].v)
        p.wait("y", s_acst, c_acst.v)

        semnames = {id(s_): n for s_, n in
                    [(s_row, "row"), (s_mm, "mm"), (s_sc, "sc"), (s_ve, "ve"),
                     (s_cst, "cst"), (s_acst, "acst"), (s_dbg, "dbg"), (s_racc, "racc")] +
                    [(SC[i], f"xc{i}") for i in range(4)] +
                    [(SW[i], f"wl{i}") for i in range(NBUF)] +
                    [(STE[i], f"te{i}") for i in range(NBUF)] +
                    [(SO[i], f"ot{i}") for i in range(2)] +
                    [(SY[i], f"yo{i}") for i in range(2)] +
                    [(SX2[i], f"x2{i}") for i in range(2)]}
        simulate(p, semnames)

        with nc.Block() as block:
            @block.tensor
            def _(t):
                for f in p.q["t"]:
                    f(t)

            @block.vector
            def _(v):
                for f in p.q["v"]:
                    f(v)

            @block.scalar
            def _(s):
                for f in p.q["s"]:
                    f(s)

            @block.sync
            def _(y):
                for f in p.q["y"]:
                    f(y)

            @block.gpsimd
            def _(g):
                for f in p.q["g"]:
                    f(g)
    return nc


def _swz(w):
    return np.ascontiguousarray(
        np.asarray(w, np.float32).reshape(16, 128, 16, 128).transpose(2, 1, 0, 3)
    ).reshape(16, 128, 2048)


def host_stage(inputs):
    f = lambda k: np.asarray(inputs[k], np.float32)
    W1, Wg, W2 = f("L0_ffn_W1"), f("L0_ffn_Wg"), f("L0_ffn_W2")
    blocks = []
    for e_ in range(E):
        if e_ == 0:
            a, b, c = f("L0_ffn_Ws1"), f("L0_ffn_Wsg"), f("L0_ffn_Ws2")
        else:
            a, b, c = W1[e_ - 1], Wg[e_ - 1], W2[e_ - 1]
        asw, bsw, csw = _swz(a), _swz(b), _swz(c)
        for hc in range(16):
            blocks.append(asw[hc])
            blocks.append(bsw[hc])
        for dc in range(16):
            blocks.append(csw[dc])
    wsw = np.ascontiguousarray(np.stack(blocks))

    gate_w = f("L0_ffn_gate_w")
    gw = np.ascontiguousarray(gate_w.reshape(16, 128, 7).transpose(1, 0, 2).reshape(128, 112))
    pad8 = lambda v: np.tile(
        np.concatenate([np.asarray(v, np.float32), [0.0]]).astype(np.float32), (128, 1))
    masks = np.zeros((4, 128, T), np.float32)
    q_idx = np.arange(128)[:, None]
    k_idx = np.arange(T)[None, :]
    for r in range(4):
        masks[r] = np.where(k_idx <= r * 128 + q_idx, 0.0, NEG).astype(np.float32)
    masks = np.ascontiguousarray(masks.transpose(1, 0, 2).reshape(128, 4 * T))

    common = dict(
        wsw=wsw,
        gw_in=gw,
        gbb_in=pad8(f("L0_ffn_gate_b")),
        bbb_in=pad8(f("L0_ffn_biases")),
        wq_in=np.ascontiguousarray(f("L0_attn_Wq").reshape(DC, 128, HD).transpose(1, 0, 2).reshape(128, DC * HD)),
        wk_in=np.ascontiguousarray(f("L0_attn_Wk").reshape(DC, 128, HD).transpose(1, 0, 2).reshape(128, DC * HD)),
        wv_in=np.ascontiguousarray(f("L0_attn_Wv").reshape(DC, 128, HD).transpose(1, 0, 2).reshape(128, DC * HD)),
        wo_in=np.ascontiguousarray(f("L0_attn_Wo")),
        bqkv_in=np.ascontiguousarray(
            np.stack([f("L0_attn_bq"), f("L0_attn_bk"), f("L0_attn_bv"),
                      np.zeros(128, np.float32)], 1)),
        bo_in=np.ascontiguousarray(f("L0_attn_bo").reshape(DC, 128).T),
        ones_in=np.ones((128, 128), np.float32),
        onescol_in=np.ones((128, 2), np.float32),
        ident_in=np.eye(128, dtype=np.float32),
        masks_in=masks,
    )
    x = np.asarray(inputs["L0_attn_x"], np.float32)
    in_maps = []
    for c in range(8):
        m = dict(common)
        m["x_in"] = np.ascontiguousarray(x[c])
        in_maps.append(m)
    return in_maps


def kernel(**inputs):
    in_maps = host_stage(inputs)
    nc = build_nc()
    res = run_bass_kernel_spmd(nc, in_maps, core_ids=list(range(8)))
    return np.stack([res.results[c]["y_out"] for c in range(8)]).astype(np.float32)


if __name__ == "__main__":
    nc = build_nc()
    print("build ok, n_inst =", sum(1 for _ in nc.all_instructions()))
